# version 10
# speedup vs baseline: 1.2716x; 1.0066x over previous
import contextlib
import numpy as np

import concourse.bass as bass
import concourse.mybir as mybir
from concourse.bass_utils import run_bass_kernel_spmd

F32 = mybir.dt.float32
BF16 = mybir.dt.bfloat16
I32 = mybir.dt.int32
AF = mybir.ActivationFunctionType
ALU = mybir.AluOpType
AX = mybir.AxisListType

NCORE = 8
B, S, D = 4, 4096, 1024
ROWS = 2048
DP = D // 128
HID = 3072
HP = HID // 128
NHC = 6
QS = 1.0 / 127.0


class SC:

    def __init__(self, handle):
        self.h = handle
        self.n = 0

    def inc(self, instr, k=1):
        instr.then_inc(self.h, k)
        self.n += k
        return instr


def _sems(nc, es, tag, names):
    return [SC(es.enter_context(nc.semaphore(f"{tag}_{n}"))) for n in names]


class SemPool:

    def __init__(self, nc, es):
        self.nc, self.es, self.m = nc, es, {}

    def __call__(self, name):
        if name not in self.m:
            self.m[name] = SC(self.es.enter_context(self.nc.semaphore(name)))
        return self.m[name]

    def take(self, names):
        return [self(n) for n in names]


def emit_consts(nc, P, s_ident, ident_dram, s_ones_c, s_ones_r,
                s_cpool=None, cvals=()):
    cs = P("cst_c")
    cs0 = cs.n
    with nc.Block() as blk:
        @blk.sync
        def _(sync):
            cs.inc(sync.dma_start(out=s_ident[:, :], in_=ident_dram[:, :]), 16)
            sync.wait_ge(cs.h, cs0 + 16)

        @blk.gpsimd
        def _(gp):
            gp.memset(s_ones_c[:, :], 1.0)
            gp.memset(s_ones_r[:, :], 1.0)
            for j, v in enumerate(cvals):
                gp.memset(s_cpool[:, j:j + 1], float(v))

def emit_load_cm(nc, P, tag, src_dram, nt, dst_bf, ident_f32,
                 xres_dram=None, col_off=0):
    lds = P.take(["ld_l0", "ld_l1", "ld_l2", "ld_l3"])
    tr, ev, fst = P.take(["ld_tr", "ld_ev", "ld_fst"])
    lb = [s.n for s in lds]
    tr0, ev0, fst0 = tr.n, ev.n, fst.n
    ntok_bf = dst_bf.shape[1] // DP
    with (
        nc.sbuf_tensor(f"{tag}_xtm", [128, 4 * 1024], F32) as s_xtm,
        nc.sbuf_tensor(f"{tag}_f32", [128, 2 * 1024], F32) as s_f32,
        nc.psum_tensor(f"{tag}_pA", [128, 512], F32) as pA,
        nc.psum_tensor(f"{tag}_pB", [128, 512], F32) as pB,
        nc.Block() as blk,
    ):
        @blk.sync
        def _(sync):
            for n in range(nt):
                sl = n % 4
                if n >= 4:
                    sync.wait_ge(tr.h, tr0 + 2 * (n - 4) + 2)
                lds[sl].inc(sync.dma_start(
                    out=s_xtm[:, sl * 1024:(sl + 1) * 1024],
                    in_=src_dram[n * 128:(n + 1) * 128, :]), 16)

        @blk.tensor
        def _(tensor):
            for n in range(nt):
                sl = n % 4
                tensor.wait_ge(lds[sl].h, lb[sl] + 16 * (n // 4 + 1))
                for half in range(2):
                    ps = pA if half == 0 else pB
                    if n > 0:
                        tensor.wait_ge(ev.h, ev0 + 2 * (n - 1) + half + 1)
                    for q in range(4):
                        dp = half * 4 + q
                        ins = tensor.transpose(
                            ps[:, q * 128:(q + 1) * 128],
                            s_xtm[:, sl * 1024 + dp * 128:
                                  sl * 1024 + (dp + 1) * 128],
                            ident_f32[:, :])
                        if q == 3:
                            tr.inc(ins, 1)

        @blk.vector
        def _(vector):
            for n in range(nt):
                ob = (n % 2) * 1024
                if xres_dram is not None and n >= 2:
                    vector.wait_ge(fst.h, fst0 + 16 * (n - 1))
                for half in range(2):
                    ps = pA if half == 0 else pB
                    vector.wait_ge(tr.h, tr0 + 2 * n + half + 1)
                    last = None
                    for q in range(4):
                        dp = half * 4 + q
                        col = col_off + n * 128
                        last = vector.tensor_copy(
                            dst_bf[:, dp * ntok_bf + col:dp * ntok_bf + col + 128],
                            ps[:, q * 128:(q + 1) * 128])
                        if xres_dram is not None:
                            last = vector.tensor_copy(
                                s_f32[:, ob + dp * 128:ob + (dp + 1) * 128],
                                ps[:, q * 128:(q + 1) * 128])
                    ev.inc(last, 1)

        if xres_dram is not None:
            @blk.gpsimd
            def _(gp):
                for n in range(nt):
                    ob = (n % 2) * 1024
                    gp.wait_ge(ev.h, ev0 + 2 * (n + 1))
                    dst = xres_dram.ap()[n].rearrange("a p c -> p a c")
                    srcv = s_f32.ap()[:, ob:ob + 1024].rearrange(
                        "p (a c) -> p a c", c=128)
                    fst.inc(gp.dma_start(out=dst, in_=srcv), 16)
                gp.wait_ge(fst.h, fst.n)

def emit_ln_cm(nc, P, tag, src_bf, ntok, eps_col, ones_col_bf, ones_row_f,
               dst=None, dst_off=0, src_off=0,
               gamma_cols=None, beta_cols=None,
               resid=None, resid_off=0):
    chunks = []
    o = 0
    while o < ntok:
        w = min(512, ntok - o)
        chunks.append((o, w))
        o += w
    nch = len(chunks)
    src_ntok = src_bf.shape[1] // DP
    dst_ntok = dst.shape[1] // DP if dst is not None else 0
    res_ntok = resid.shape[1] // DP if resid is not None else 0
    sq, mm, row, rs, bc, nf, so = P.take(
        ["ln_sq", "ln_mm", "ln_row", "ln_rs", "ln_bc", "ln_nf", "ln_so"])
    sq0, mm0, row0, rs0, bc0, nf0, so0 = (
        sq.n, mm.n, row.n, rs.n, bc.n, nf.n, so.n)
    with (
        nc.sbuf_tensor(f"{tag}_sq", [128, 1024], BF16) as s_sq,
        nc.sbuf_tensor(f"{tag}_mu", [1, ntok], F32) as s_mu,
        nc.sbuf_tensor(f"{tag}_var", [1, ntok], F32) as s_var,
        nc.sbuf_tensor(f"{tag}_rstd", [1, ntok], F32) as s_rstd,
        nc.sbuf_tensor(f"{tag}_lnr", [1, 512], F32) as s_lnr,
        nc.sbuf_tensor(f"{tag}_musq", [1, 512], F32) as s_musq,
        nc.sbuf_tensor(f"{tag}_nrm", [128, 512], F32) as s_nrm,
        nc.psum_tensor(f"{tag}_pm", [1, 512], F32) as pm,
        nc.psum_tensor(f"{tag}_pq", [1, 512], F32) as pq,
        nc.psum_tensor(f"{tag}_pbm", [128, 1024], F32) as pbm,
        nc.psum_tensor(f"{tag}_pbr", [128, 1024], F32) as pbr,
        nc.Block() as blk,
    ):
        def ssl(dp, tc):
            o, w = chunks[tc]
            c = src_off + o
            return src_bf[:, dp * src_ntok + c:dp * src_ntok + c + w]

        @blk.scalar
        def _(scalar):
            for tc in range(nch):
                o, w = chunks[tc]
                for dp in range(DP):
                    if dp >= 2:
                        scalar.wait_ge(mm.h, mm0 + tc * (DP + 1) + dp)
                    elif tc >= 1:
                        scalar.wait_ge(mm.h, mm0 + (tc - 1) * (DP + 1) + 8 + dp)
                    sq.inc(scalar.activation(
                        s_sq[:, (dp % 2) * 512:(dp % 2) * 512 + w],
                        ssl(dp, tc), AF.Square), 1)
                scalar.wait_ge(row.h, row0 + tc * 3 + 3)
                so.inc(scalar.activation(
                    s_lnr[:, 0:w], s_var[:, o:o + w], AF.Ln, bias=eps_col), 1)
                scalar.wait_ge(so.h, so.n)
                rs.inc(scalar.activation(
                    s_rstd[:, o:o + w], s_lnr[:, 0:w], AF.Exp, scale=-0.5), 1)

        @blk.tensor
        def _(tensor):
            for tc in range(nch):
                o, w = chunks[tc]
                if tc >= 1:
                    tensor.wait_ge(row.h, row0 + (tc - 1) * 3 + 3)
                for dp in range(DP):
                    ins = tensor.matmul(pm[:, 0:w], ones_col_bf[:, :],
                                        ssl(dp, tc),
                                        start=(dp == 0), stop=(dp == DP - 1))
                    if dp == DP - 1:
                        mm.inc(ins, 1)
                for dp in range(DP):
                    tensor.wait_ge(sq.h, sq0 + tc * DP + dp + 1)
                    mm.inc(tensor.matmul(
                        pq[:, 0:w], ones_col_bf[:, :],
                        s_sq[:, (dp % 2) * 512:(dp % 2) * 512 + w],
                        start=(dp == 0), stop=(dp == DP - 1)), 1)
            for tc in range(nch):
                o, w = chunks[tc]
                tensor.wait_ge(rs.h, rs0 + tc + 1)
                if tc >= 2:
                    tensor.wait_ge(nf.h, nf0 + tc - 1)
                par = (tc % 2) * 512
                tensor.matmul(pbm[:, par:par + w], ones_row_f[:, :],
                              s_mu[:, o:o + w], start=True, stop=True)
                bc.inc(tensor.matmul(pbr[:, par:par + w], ones_row_f[:, :],
                                     s_rstd[:, o:o + w],
                                     start=True, stop=True), 1)

        @blk.vector
        def _(vector):
            for tc in range(nch):
                o, w = chunks[tc]
                vector.wait_ge(mm.h, mm0 + tc * (DP + 1) + 1)
                row.inc(vector.tensor_scalar(
                    s_mu[:, o:o + w], pm[:, 0:w], 1.0 / D, None,
                    op0=ALU.mult), 1)
                vector.wait_ge(mm.h, mm0 + (tc + 1) * (DP + 1))
                row.inc(vector.tensor_scalar(
                    s_var[:, o:o + w], pq[:, 0:w], 1.0 / D, None,
                    op0=ALU.mult), 1)
                vector.tensor_tensor(s_musq[:, 0:w], s_mu[:, o:o + w],
                                     s_mu[:, o:o + w], ALU.mult)
                row.inc(vector.tensor_tensor(
                    s_var[:, o:o + w], s_var[:, o:o + w],
                    s_musq[:, 0:w], ALU.subtract), 1)
            for tc in range(nch):
                o, w = chunks[tc]
                vector.wait_ge(bc.h, bc0 + tc + 1)
                par = (tc % 2) * 512
                last = None
                for dp in range(DP):
                    vector.tensor_tensor(s_nrm[:, 0:w], ssl(dp, tc),
                                         pbm[:, par:par + w], ALU.subtract)
                    if gamma_cols is None and resid is None:
                        dc = dst_off + o
                        out = dst[:, dp * dst_ntok + dc:dp * dst_ntok + dc + w]
                        last = vector.tensor_tensor(out, s_nrm[:, 0:w],
                                                    pbr[:, par:par + w],
                                                    ALU.mult)
                    else:
                        last = vector.tensor_tensor(s_nrm[:, 0:w],
                                                    s_nrm[:, 0:w],
                                                    pbr[:, par:par + w],
                                                    ALU.mult)
                        if gamma_cols is not None:
                            if resid is None:
                                dc = dst_off + o
                                tgt = dst[:, dp * dst_ntok + dc:
                                          dp * dst_ntok + dc + w]
                                last = vector.tensor_scalar(
                                    tgt, s_nrm[:, 0:w],
                                    gamma_cols[dp], beta_cols[dp],
                                    op0=ALU.mult, op1=ALU.add)
                            else:
                                last = vector.tensor_scalar(
                                    s_nrm[:, 0:w], s_nrm[:, 0:w],
                                    gamma_cols[dp], beta_cols[dp],
                                    op0=ALU.mult, op1=ALU.add)
                        if resid is not None:
                            rc = resid_off + o
                            rsl = resid[:, dp * res_ntok + rc:
                                        dp * res_ntok + rc + w]
                            last = vector.tensor_tensor(rsl, rsl,
                                                        s_nrm[:, 0:w], ALU.add)
                nf.inc(last, 1)

def emit_reload_tm(nc, P, tag, src_dram, s_xtm, nt):
    ld = P("rl_ld")
    ld0 = ld.n
    with nc.Block() as blk:
        @blk.sync
        def _(sync):
            for n in range(nt):
                ld.inc(sync.dma_start(
                    out=s_xtm[:, n * 1024:(n + 1) * 1024],
                    in_=src_dram[n * 128:(n + 1) * 128, :]), 16)
            sync.wait_ge(ld.h, ld.n)

def emit_store_tm(nc, P, tag, src_cm_f32, dst_dram, nt, ident_f32,
                  add_tm=None):
    ntok = src_cm_f32.shape[1] // DP
    tr, ev, st0_, st1_ = P.take(["st_tr", "st_ev", "st_s0", "st_s1"])
    st = [st0_, st1_]
    tr0, ev0 = tr.n, ev.n
    sb = [s.n for s in st]
    with (
        nc.sbuf_tensor(f"{tag}_o", [128, 2 * 1024], F32) as s_out,
        nc.psum_tensor(f"{tag}_pA", [128, 512], F32) as pA,
        nc.psum_tensor(f"{tag}_pB", [128, 512], F32) as pB,
        nc.Block() as blk,
    ):
        @blk.tensor
        def _(tensor):
            for n in range(nt):
                for half in range(2):
                    ps = pA if half == 0 else pB
                    if n > 0:
                        tensor.wait_ge(ev.h, ev0 + 2 * (n - 1) + half + 1)
                    for q in range(4):
                        dp = half * 4 + q
                        ins = tensor.transpose(
                            ps[:, q * 128:(q + 1) * 128],
                            src_cm_f32[:, dp * ntok + n * 128:
                                       dp * ntok + (n + 1) * 128],
                            ident_f32[:, :])
                        if q == 3:
                            tr.inc(ins, 1)

        @blk.vector
        def _(vector):
            for n in range(nt):
                ob = (n % 2) * 1024
                if n >= 2:
                    vector.wait_ge(st[n % 2].h,
                                   sb[n % 2] + 16 * ((n - 2) // 2 + 1))
                for half in range(2):
                    ps = pA if half == 0 else pB
                    vector.wait_ge(tr.h, tr0 + 2 * n + half + 1)
                    last = None
                    for q in range(4):
                        dp = half * 4 + q
                        osl = s_out[:, ob + dp * 128:ob + (dp + 1) * 128]
                        if add_tm is not None:
                            last = vector.tensor_tensor(
                                osl,
                                add_tm[:, n * 1024 + dp * 128:
                                       n * 1024 + (dp + 1) * 128],
                                ps[:, q * 128:(q + 1) * 128], ALU.add)
                        else:
                            last = vector.tensor_copy(
                                osl, ps[:, q * 128:(q + 1) * 128])
                    ev.inc(last, 1)

        @blk.sync
        def _(sync):
            for n in range(nt):
                sync.wait_ge(ev.h, ev0 + 2 * (n + 1))
                ob = (n % 2) * 1024
                st[n % 2].inc(sync.dma_start(
                    out=dst_dram[n * 128:(n + 1) * 128, :],
                    in_=s_out[:, ob:ob + 1024]), 16)
            sync.wait_ge(st[0].h, st[0].n)
            sync.wait_ge(st[1].h, st[1].n)

def emit_l0ffn_body(nc, P, s_xln, s_ffn, wgT, wuT, wdT,
                    xln_off=0, ffn_off=0, ntok=ROWS, tag="ffn"):
    assert ntok % 1024 == 0
    nsw = ntok // 1024
    xln_ntok = s_xln.shape[1] // DP
    ffn_ntok = s_ffn.shape[1] // DP
    wdld, wldA, wldB, gmm, umm, sact, hev, dmm, dev = P.take(
        ["ff_wd", "ff_wa", "ff_wb", "ff_gm", "ff_um", "ff_sa", "ff_he",
         "ff_dm", "ff_de"])
    wld = [wldA, wldB]
    wb = [wldA.n, wldB.n]
    wd0, gm0, um0, sa0, he0, dm0, de0 = (wdld.n, gmm.n, umm.n, sact.n,
                                         hev.n, dmm.n, dev.n)
    with (
        nc.sbuf_tensor(f"{tag}_wd", [128, HP * 1024], BF16) as s_wd,
        nc.sbuf_tensor(f"{tag}_hdn", [128, HP * 1024], BF16) as s_hdn,
        nc.sbuf_tensor(f"{tag}_wg", [128, 2 * 4096], BF16) as s_wg,
        nc.sbuf_tensor(f"{tag}_wu", [128, 2 * 4096], BF16) as s_wu,
        nc.sbuf_tensor(f"{tag}_silu", [128, 3 * 512], F32) as s_silu,
        nc.psum_tensor(f"{tag}_pg", [128, 1536], F32) as pg,
        nc.psum_tensor(f"{tag}_pu", [128, 1536], F32) as pu,
        nc.psum_tensor(f"{tag}_pd", [128, 1024], F32) as pd,
        nc.Block() as blk,
    ):
        wgTr = wgT.ap().rearrange("(a p) h -> p a h", p=128)
        wuTr = wuT.ap().rearrange("(a p) h -> p a h", p=128)
        wdTr = wdT.ap().rearrange("(a p) d -> p a d", p=128)
        s_wg_r = s_wg.ap().rearrange("p (b a h) -> p b a h", b=2, a=8)
        s_wu_r = s_wu.ap().rearrange("p (b a h) -> p b a h", b=2, a=8)
        s_wd_r = s_wd.ap().rearrange("p (a d) -> p a d", d=1024)

        @blk.gpsimd
        def _(gp):
            for part in range(4):
                wdld.inc(gp.dma_start(out=s_wd_r[:, part * 6:(part + 1) * 6],
                                      in_=wdTr[:, part * 6:(part + 1) * 6]),
                         16)
            for it in range(nsw * NHC):
                c = it % NHC
                if it >= 2:
                    gp.wait_ge(umm.h, um0 + 8 * (it - 1))
                bsl = it % 2
                wld[bsl].inc(gp.dma_start(
                    out=s_wg_r[:, bsl],
                    in_=wgTr[:, :, c * 512:(c + 1) * 512]), 16)
                wld[bsl].inc(gp.dma_start(
                    out=s_wu_r[:, bsl],
                    in_=wuTr[:, :, c * 512:(c + 1) * 512]), 16)

        @blk.tensor
        def _(tensor):
            nj = 0
            nd = 0
            for sw in range(nsw):
                for c in range(NHC):
                    it = sw * NHC + c
                    bsl = it % 2
                    tensor.wait_ge(wld[bsl].h, wb[bsl] + 32 * (it // 2 + 1))
                    for hp in range(4):
                        for tc in range(2):
                            tok = xln_off + sw * 1024 + tc * 512
                            par = (nj % 3) * 512
                            if nj >= 3:
                                tensor.wait_ge(sact.h, sa0 + nj - 2)
                                tensor.wait_ge(hev.h, he0 + nj - 2)
                            for dp in range(DP):
                                ins = tensor.matmul(
                                    pg[:, par:par + 512],
                                    s_wg[:, bsl * 4096 + dp * 512 + hp * 128:
                                         bsl * 4096 + dp * 512 + (hp + 1) * 128],
                                    s_xln[:, dp * xln_ntok + tok:
                                          dp * xln_ntok + tok + 512],
                                    start=(dp == 0), stop=(dp == DP - 1))
                            gmm.inc(ins, 1)
                            for dp in range(DP):
                                ins = tensor.matmul(
                                    pu[:, par:par + 512],
                                    s_wu[:, bsl * 4096 + dp * 512 + hp * 128:
                                         bsl * 4096 + dp * 512 + (hp + 1) * 128],
                                    s_xln[:, dp * xln_ntok + tok:
                                          dp * xln_ntok + tok + 512],
                                    start=(dp == 0), stop=(dp == DP - 1))
                            umm.inc(ins, 1)
                            nj += 1
                tensor.wait_ge(wdld.h, wd0 + 64)
                tensor.wait_ge(hev.h, he0 + (sw + 1) * 48)
                for tc in range(2):
                    for dp in range(DP):
                        par = (nd % 2) * 512
                        if nd >= 2:
                            tensor.wait_ge(dev.h, de0 + nd - 1)
                        for h24 in range(HP):
                            ins = tensor.matmul(
                                pd[:, par:par + 512],
                                s_wd[:, h24 * 1024 + dp * 128:
                                     h24 * 1024 + (dp + 1) * 128],
                                s_hdn[:, h24 * 1024 + tc * 512:
                                      h24 * 1024 + (tc + 1) * 512],
                                start=(h24 == 0), stop=(h24 == HP - 1))
                        dmm.inc(ins, 1)
                        nd += 1

        @blk.scalar
        def _(scalar):
            nj = 0
            nd = 0
            for sw in range(nsw):
                for it in range(NHC):
                    for j in range(8):
                        par = (nj % 3) * 512
                        scalar.wait_ge(gmm.h, gm0 + nj + 1)
                        sact.inc(scalar.activation(
                            s_silu[:, par:par + 512], pg[:, par:par + 512],
                            AF.Silu, scale=QS), 1)
                        nj += 1
                for j in range(16):
                    par = (nd % 2) * 512
                    scalar.wait_ge(dmm.h, dm0 + nd + 1)
                    tc, dp = j // 8, j % 8
                    out = s_ffn[:, dp * ffn_ntok + ffn_off + sw * 1024 +
                                tc * 512:
                                dp * ffn_ntok + ffn_off + sw * 1024 +
                                tc * 512 + 512]
                    dev.inc(scalar.mul(out, pd[:, par:par + 512], QS * QS), 1)
                    nd += 1

        @blk.vector
        def _(vector):
            nj = 0
            for sw in range(nsw):
                for it in range(NHC):
                    c = it
                    for hp in range(4):
                        for tc in range(2):
                            par = (nj % 3) * 512
                            vector.wait_ge(sact.h, sa0 + nj + 1)
                            vector.wait_ge(umm.h, um0 + nj + 1)
                            h24 = 4 * c + hp
                            out = s_hdn[:, h24 * 1024 + tc * 512:
                                        h24 * 1024 + (tc + 1) * 512]
                            hev.inc(vector.tensor_tensor(
                                out, s_silu[:, par:par + 512],
                                pu[:, par:par + 512], ALU.mult), 1)
                            nj += 1

def build_l0ffn():
    nc = bass.Bass("TRN2", target_bir_lowering=False,
                   detect_race_conditions=False)
    xin = nc.dram_tensor("xin", [ROWS, D], F32, kind="ExternalInput")
    wgT = nc.dram_tensor("wgT", [D, HID], I32, kind="ExternalInput")
    wuT = nc.dram_tensor("wuT", [D, HID], I32, kind="ExternalInput")
    wdT = nc.dram_tensor("wdT", [HID, D], I32, kind="ExternalInput")
    ident = nc.dram_tensor("ident", [128, 128], F32, kind="ExternalInput")
    xout = nc.dram_tensor("xout", [ROWS, D], F32, kind="ExternalOutput")
    NT = ROWS // 128

    with contextlib.ExitStack() as es:
        P = SemPool(nc, es)
        E = es.enter_context
        s_ident = E(nc.sbuf_tensor("s_ident", [128, 128], F32))
        s_ones_c = E(nc.sbuf_tensor("ones_c", [128, 1], BF16))
        s_ones_r = E(nc.sbuf_tensor("ones_r", [1, 128], F32))
        s_cpool = E(nc.sbuf_tensor("cpool", [128, 4], F32))
        s_ffn = E(nc.sbuf_tensor("ffnout", [128, DP * ROWS], BF16))

        emit_consts(nc, P, s_ident, ident, s_ones_c, s_ones_r,
                    s_cpool, [1e-5, 1e-3])
        eps5 = s_cpool[0:1, 0:1]

        with (
            nc.sbuf_tensor("xln", [128, DP * ROWS], BF16) as s_xln,
        ):
            with nc.sbuf_tensor("xbf", [128, DP * ROWS], BF16) as s_xbf:
                emit_load_cm(nc, P, "ld", xin, NT, s_xbf, s_ident)
                emit_ln_cm(nc, P, "ln1", s_xbf, ROWS, eps5,
                           s_ones_c, s_ones_r, dst=s_xln)
            emit_l0ffn_body(nc, P, s_xln, s_ffn, wgT, wuT, wdT)

        with nc.sbuf_tensor("l2f", [128, DP * ROWS], F32) as s_l2f:
            emit_ln_cm(nc, P, "ln2", s_ffn, ROWS, eps5,
                       s_ones_c, s_ones_r, dst=s_l2f)
            with nc.sbuf_tensor("xtm2", [128, NT * 1024], F32) as s_xtm2:
                emit_reload_tm(nc, P, "rl", xin, s_xtm2, NT)
                emit_store_tm(nc, P, "st", s_l2f, xout, NT, s_ident,
                              add_tm=s_xtm2)
    return nc


_IDENT = np.eye(128, dtype=np.float32)


def l0ffn_inmaps(Xstate, w_gate, w_up, w_down):
    wgT = np.ascontiguousarray(w_gate.T)
    wuT = np.ascontiguousarray(w_up.T)
    wdT = np.ascontiguousarray(w_down.T)
    maps = []
    for c in range(NCORE):
        b, h = c // 2, c % 2
        maps.append({
            "xin": np.ascontiguousarray(Xstate[b, h * ROWS:(h + 1) * ROWS]),
            "wgT": wgT, "wuT": wuT, "wdT": wdT, "ident": _IDENT,
        })
    return maps


def run_l0ffn(Xstate, w_gate, w_up, w_down):
    nc = build_l0ffn()
    res = run_bass_kernel_spmd(nc, l0ffn_inmaps(Xstate, w_gate, w_up, w_down),
                               core_ids=list(range(NCORE)))
    out = np.empty_like(Xstate)
    for c in range(NCORE):
        b, h = c // 2, c % 2
        out[b, h * ROWS:(h + 1) * ROWS] = res.results[c]["xout"]
    return out


def emit_proj(nc, P, tag, s_w, src_bf, dst, ntok, mode,
              src_off=0, dst_off=0, rbuf=None, rbuf_off=0):
    assert ntok % 512 == 0
    ntc = ntok // 512
    src_ntok = src_bf.shape[1] // DP
    dst_ntok = dst.shape[1] // DP
    r_ntok = rbuf.shape[1] // DP if rbuf is not None else 0
    pmm, pev, ract, tfree = P.take(["pj_mm", "pj_ev", "pj_ra", "pj_tf"])
    pmm0, pev0, ract0, tfree0 = pmm.n, pev.n, ract.n, tfree.n
    with (
        nc.psum_tensor(f"{tag}_pd", [128, 2048], F32) as pd,
        nc.sbuf_tensor(f"{tag}_tmp", [128, 2048], F32) as s_tmp,
        nc.Block() as blk,
    ):
        def jobs():
            j = 0
            for dpo in range(DP):
                for tc in range(ntc):
                    yield j, dpo, tc, (j % 4) * 512
                    j += 1

        @blk.tensor
        def _(tensor):
            for j, dpo, tc, par in jobs():
                if j >= 4:
                    tensor.wait_ge(pev.h, pev0 + j - 3)
                for dpi in range(DP):
                    ins = tensor.matmul(
                        pd[:, par:par + 512],
                        s_w[:, dpi * 1024 + dpo * 128:dpi * 1024 + (dpo + 1) * 128],
                        src_bf[:, dpi * src_ntok + src_off + tc * 512:
                               dpi * src_ntok + src_off + tc * 512 + 512],
                        start=(dpi == 0), stop=(dpi == DP - 1))
                pmm.inc(ins, 1)

        if mode == "sigmoid":
            @blk.scalar
            def _(scalar):
                for j, dpo, tc, par in jobs():
                    scalar.wait_ge(pmm.h, pmm0 + j + 1)
                    out = dst[:, dpo * dst_ntok + dst_off + tc * 512:
                              dpo * dst_ntok + dst_off + tc * 512 + 512]
                    pev.inc(scalar.activation(out, pd[:, par:par + 512],
                                              AF.Sigmoid), 1)
        elif mode == "copy":
            @blk.scalar
            def _(scalar):
                for j, dpo, tc, par in jobs():
                    scalar.wait_ge(pmm.h, pmm0 + j + 1)
                    out = dst[:, dpo * dst_ntok + dst_off + tc * 512:
                              dpo * dst_ntok + dst_off + tc * 512 + 512]
                    pev.inc(scalar.copy(out, pd[:, par:par + 512]), 1)
        elif mode == "relusq":
            @blk.scalar
            def _(scalar):
                for j, dpo, tc, par in jobs():
                    scalar.wait_ge(pmm.h, pmm0 + j + 1)
                    if j >= 4:
                        scalar.wait_ge(tfree.h, tfree0 + j - 3)
                    pev.inc(scalar.activation(s_tmp[:, par:par + 512],
                                              pd[:, par:par + 512],
                                              AF.Relu), 1)

            @blk.vector
            def _(vector):
                for j, dpo, tc, par in jobs():
                    vector.wait_ge(pev.h, pev0 + j + 1)
                    out = dst[:, dpo * dst_ntok + dst_off + tc * 512:
                              dpo * dst_ntok + dst_off + tc * 512 + 512]
                    tfree.inc(vector.tensor_tensor(
                        out, s_tmp[:, par:par + 512], s_tmp[:, par:par + 512],
                        ALU.mult), 1)
        elif mode == "rmult":
            @blk.vector
            def _(vector):
                for j, dpo, tc, par in jobs():
                    vector.wait_ge(pmm.h, pmm0 + j + 1)
                    out = dst[:, dpo * dst_ntok + dst_off + tc * 512:
                              dpo * dst_ntok + dst_off + tc * 512 + 512]
                    rs = rbuf[:, dpo * r_ntok + rbuf_off + tc * 512:
                              dpo * r_ntok + rbuf_off + tc * 512 + 512]
                    pev.inc(vector.tensor_tensor(out, rs, pd[:, par:par + 512],
                                                 ALU.mult), 1)
        else:
            raise ValueError(mode)

def emit_cvecs(nc, P, tag, cvecs_dram, s_cvc, V, W, ident_f32):
    ld, tr, ev = P.take(["cv_ld", "cv_tr", "cv_ev"])
    ld0, tr0, ev0 = ld.n, tr.n, ev.n
    with (
        nc.sbuf_tensor(f"{tag}_rows", [V, 1024], F32) as s_rows,
        nc.psum_tensor(f"{tag}_pc", [128, 512], F32) as pc,
        nc.Block() as blk,
    ):
        @blk.sync
        def _(sync):
            ld.inc(sync.dma_start(out=s_rows[:, :], in_=cvecs_dram[:, :]), 16)

        @blk.tensor
        def _(tensor):
            tensor.wait_ge(ld.h, ld0 + 16)
            for dp in range(DP):
                if dp >= 2:
                    tensor.wait_ge(ev.h, ev0 + dp - 1)
                tr.inc(tensor.transpose(
                    pc[:, (dp % 2) * V:(dp % 2) * V + V],
                    s_rows[:, dp * 128:(dp + 1) * 128],
                    ident_f32[0:V, 0:V]), 1)

        @blk.vector
        def _(vector):
            for dp in range(DP):
                vector.wait_ge(tr.h, tr0 + dp + 1)
                ev.inc(vector.tensor_copy(
                    s_cvc[:, dp * W:dp * W + V],
                    pc[:, (dp % 2) * V:(dp % 2) * V + V]), 1)

def emit_wload(nc, gp, wdram, s_w, sem):
    src = wdram.ap().rearrange("(a p) d -> p a d", p=128)
    dst = s_w.ap().rearrange("p (a d) -> p a d", d=1024)
    sem.inc(gp.dma_start(out=dst, in_=src), 16)


PAD4 = 512
NTOK4 = PAD4 + ROWS


def build_l1ffn():
    nc = bass.Bass("TRN2", target_bir_lowering=False,
                   detect_race_conditions=False)
    xin = nc.dram_tensor("xin", [NTOK4, D], F32, kind="ExternalInput")
    wrT = nc.dram_tensor("wrT", [D, D], F32, kind="ExternalInput")
    wkT = nc.dram_tensor("wkT", [D, D], F32, kind="ExternalInput")
    wvT = nc.dram_tensor("wvT", [D, D], F32, kind="ExternalInput")
    cvecs = nc.dram_tensor("cvecs", [2, 1024], F32, kind="ExternalInput")
    ident = nc.dram_tensor("ident", [128, 128], F32, kind="ExternalInput")
    xout = nc.dram_tensor("xout", [ROWS, D], F32, kind="ExternalOutput")
    NT = NTOK4 // 128

    with contextlib.ExitStack() as es:
        P = SemPool(nc, es)
        E = es.enter_context
        s_ident = E(nc.sbuf_tensor("s_ident", [128, 128], F32))
        s_ones_c = E(nc.sbuf_tensor("ones_c", [128, 1], BF16))
        s_ones_r = E(nc.sbuf_tensor("ones_r", [1, 128], F32))
        s_cpool = E(nc.sbuf_tensor("cpool", [128, 4], F32))
        s_cvc = E(nc.sbuf_tensor("cvc", [128, DP * 4], F32))
        s_ffn = E(nc.sbuf_tensor("ffnout", [128, DP * ROWS], BF16))

        emit_consts(nc, P, s_ident, ident, s_ones_c, s_ones_r,
                    s_cpool, [1e-5, 1e-3])
        eps5 = s_cpool[0:1, 0:1]
        emit_cvecs(nc, P, "cv", cvecs, s_cvc, 2, 4, s_ident)

        def mu_r(dp):
            return s_cvc[:, dp * 4:dp * 4 + 1]

        def mu_k(dp):
            return s_cvc[:, dp * 4 + 1:dp * 4 + 2]

        def om_r(dp):
            return s_cvc[:, dp * 4 + 2:dp * 4 + 3]

        def om_k(dp):
            return s_cvc[:, dp * 4 + 3:dp * 4 + 4]

        with nc.sbuf_tensor("xk", [128, DP * ROWS], BF16) as s_xk:
            es_xr = contextlib.ExitStack()
            s_xr = es_xr.enter_context(
                nc.sbuf_tensor("xr", [128, DP * ROWS], BF16, side="right"))
            with nc.sbuf_tensor("xln", [128, DP * NTOK4], BF16,
                                side="right") as s_xln:
                emit_load_cm(nc, P, "ld", xin, NT, s_xln, s_ident)
                emit_ln_cm(nc, P, "ln1", s_xln, NTOK4, eps5,
                           s_ones_c, s_ones_r, dst=s_xln)
                with (
                    nc.sbuf_tensor("mixtmp", [128, 512], F32) as s_mt,
                    nc.Block() as blk,
                ):
                    @blk.vector
                    def _(vector):
                        for dp in range(DP):
                            vector.tensor_scalar(s_cvc[:, dp * 4 + 2:dp * 4 + 4],
                                                 s_cvc[:, dp * 4:dp * 4 + 2],
                                                 -1.0, 1.0, op0=ALU.mult,
                                                 op1=ALU.add)
                        for dp in range(DP):
                            for tc in range(ROWS // 512):
                                cur = s_xln[:, dp * NTOK4 + PAD4 + tc * 512:
                                            dp * NTOK4 + PAD4 + tc * 512 + 512]
                                prv = s_xln[:, dp * NTOK4 + PAD4 - 1 + tc * 512:
                                            dp * NTOK4 + PAD4 - 1 + tc * 512 + 512]
                                for (mu, om, dstb) in ((mu_r, om_r, s_xr),
                                                       (mu_k, om_k, s_xk)):
                                    vector.tensor_scalar(
                                        s_mt[:, :], prv, om(dp), None,
                                        op0=ALU.mult)
                                    vector.scalar_tensor_tensor(
                                        dstb[:, dp * ROWS + tc * 512:
                                             dp * ROWS + tc * 512 + 512],
                                        cur, mu(dp), s_mt[:, :],
                                        op0=ALU.mult, op1=ALU.add)

            with (
                nc.sbuf_tensor("wr", [128, DP * 1024], BF16) as s_wr,
                nc.sbuf_tensor("wk", [128, DP * 1024], BF16) as s_wk,
                nc.sbuf_tensor("wv", [128, DP * 1024], BF16) as s_wv,
            ):
                wl = P.take(["wl_w"])[0]
                with nc.Block() as blk:
                    @blk.gpsimd
                    def _(gp):
                        emit_wload(nc, gp, wrT, s_wr, wl)
                        emit_wload(nc, gp, wkT, s_wk, wl)
                        emit_wload(nc, gp, wvT, s_wv, wl)
                        gp.wait_ge(wl.h, wl.n)

                with nc.sbuf_tensor("rbuf", [128, DP * ROWS], BF16) as s_r:
                    emit_proj(nc, P, "pr", s_wr, s_xr, s_r, ROWS, "sigmoid")
                    es_xr.close()
                    with nc.sbuf_tensor("kact", [128, DP * ROWS], BF16) as s_ka:
                        emit_proj(nc, P, "pk", s_wk, s_xk, s_ka, ROWS,
                                  "relusq")
                        emit_proj(nc, P, "pv", s_wv, s_ka, s_ffn, ROWS,
                                  "rmult", rbuf=s_r)

        with nc.sbuf_tensor("l2f", [128, DP * ROWS], F32) as s_l2f:
            emit_ln_cm(nc, P, "ln2", s_ffn, ROWS, eps5,
                       s_ones_c, s_ones_r, dst=s_l2f)
            with nc.sbuf_tensor("xtm2", [128, (ROWS // 128) * 1024], F32) as s_xtm2:
                emit_reload_tm(nc, P, "rl", xin[PAD4:, :], s_xtm2, ROWS // 128)
                emit_store_tm(nc, P, "st", s_l2f, xout, ROWS // 128, s_ident,
                              add_tm=s_xtm2)
    return nc


def l1ffn_inmaps(Xstate, W_r, W_k, W_v, mu_r, mu_k):
    wrT = np.ascontiguousarray(W_r.T)
    wkT = np.ascontiguousarray(W_k.T)
    wvT = np.ascontiguousarray(W_v.T)
    cv = np.stack([mu_r, mu_k]).astype(np.float32)
    maps = []
    for c in range(NCORE):
        b, h = c // 2, c % 2
        xin = np.zeros((NTOK4, D), np.float32)
        lo = h * ROWS - PAD4
        src_lo = max(lo, 0)
        xin[src_lo - lo:] = Xstate[b, src_lo:h * ROWS + ROWS]
        maps.append({"xin": xin, "wrT": wrT, "wkT": wkT, "wvT": wvT,
                     "cvecs": cv, "ident": _IDENT})
    return maps


def run_l1ffn(Xstate, W_r, W_k, W_v, mu_r, mu_k):
    nc = build_l1ffn()
    res = run_bass_kernel_spmd(nc, l1ffn_inmaps(Xstate, W_r, W_k, W_v,
                                                mu_r, mu_k),
                               core_ids=list(range(NCORE)))
    out = np.empty_like(Xstate)
    for c in range(NCORE):
        b, h = c // 2, c % 2
        out[b, h * ROWS:(h + 1) * ROWS] = res.results[c]["xout"]
    return out


NG = 4
GH = 4


def _emit_l1attn_group(nc, es, g, B, sems, qw=ROWS, zw=ROWS,
                       extra=False):
    s = B
    pj, pje, bve, sqa, smm, rmx, nsel, bsm, bex, bpv, lns, bcm, rbe, bze = sems

    kjobs = [("k", a, tc) for a in range(2) for tc in range(S // 512)]
    qjobs = [("q", a, tc) for a in range(2) for tc in range(ROWS // 512)]
    vjobs = [("v", t, 0) for t in range(32)]
    xjobs = [("x", a, 0) for a in range(2)] if extra else []
    jobs = kjobs + qjobs + vjobs + xjobs
    stat_steps = ([("q", tc) for tc in range(ROWS // 512)]
                  + [("k", tc) for tc in range(S // 512)])
    pj0, pje0, bve0, sqa0, smm0, rmx0, nsel0 = (
        pj.n, pje.n, bve.n, sqa.n, smm.n, rmx.n, nsel.n)
    with (
        nc.psum_tensor(f"g{g}_pp", [128, 1024], F32) as pp,
        nc.psum_tensor(f"g{g}_pn", [1, 512], F32) as pn,
        nc.psum_tensor(f"g{g}_ps", [128, 1], F32) as ps,
        nc.Block() as blk,
    ):
        @blk.tensor
        def _(tensor):
            pj.inc(tensor.matmul(pp[:, 0:256], s["ones_r"][:, :],
                                 s["bvr"][:, g * 256:(g + 1) * 256],
                                 start=True, stop=True), 1)
            for j, (kind, a, tc) in enumerate(jobs):
                par = (j % 2) * 512
                if j < 2:
                    tensor.wait_ge(bve.h, bve0 + 1)
                else:
                    tensor.wait_ge(pje.h, pje0 + j - 1)
                if kind == "k":
                    for dpi in range(DP):
                        ins = tensor.matmul(
                            pp[:, par:par + 512],
                            s["wk"][:, dpi * 1024 + g * 256 + a * 128:
                                    dpi * 1024 + g * 256 + (a + 1) * 128],
                            s["xln"][:, dpi * S + tc * 512:
                                     dpi * S + tc * 512 + 512],
                            start=(dpi == 0), stop=(dpi == DP - 1))
                elif kind == "q":
                    for dpi in range(DP):
                        ins = tensor.matmul(
                            pp[:, par:par + 512],
                            s["wq"][:, dpi * 1024 + g * 256 + a * 128:
                                    dpi * 1024 + g * 256 + (a + 1) * 128],
                            s["xln"][:, dpi * S + tc * 512:
                                     dpi * S + tc * 512 + 512],
                            start=(dpi == 0), stop=(dpi == DP - 1))
                elif kind == "v":
                    t = a
                    for dpi in range(DP):
                        ins = tensor.matmul(
                            pp[:, par:par + 256],
                            s["xln"][:, dpi * S + t * 128:
                                     dpi * S + (t + 1) * 128],
                            s["wv"][:, dpi * 1024 + g * 256:
                                    dpi * 1024 + (g + 1) * 256],
                            start=(dpi == 0), stop=(dpi == DP - 1))
                else:
                    for dpi in range(DP):
                        ins = tensor.matmul(
                            pp[:, par:par + 128],
                            s["wq"][:, dpi * 1024 + g * 256 + a * 128:
                                    dpi * 1024 + g * 256 + (a + 1) * 128],
                            s["xln"][:, dpi * S + S - 128:dpi * S + S],
                            start=(dpi == 0), stop=(dpi == DP - 1))
                pj.inc(ins, 1)
            for st, (kind, tc) in enumerate(stat_steps):
                tensor.wait_ge(sqa.h, sqa0 + 2 * st + 2)
                if st >= 1:
                    tensor.wait_ge(rmx.h, rmx0 + st)
                for a in range(2):
                    ins = tensor.matmul(pn[:, :], s["ones_c"][:, :],
                                        s["sqs"][:, a * 512:(a + 1) * 512],
                                        start=(a == 0), stop=(a == 1))
                smm.inc(ins, 1)
            tensor.wait_ge(nsel.h, nsel0 + 1)
            smm.inc(tensor.matmul(ps[:, :], s["ones_r"][:, :],
                                  s["mrow"][0:1, 3:4], start=True, stop=True),
                    1)

        @blk.scalar
        def _(scalar):
            scalar.wait_ge(pj.h, pj0 + 1)
            bve.inc(scalar.copy(s["bvb"][:, :], pp[:, 0:256]), 1)
            for st, (kind, tc) in enumerate(stat_steps):
                for a in range(2):
                    if kind == "q":
                        src = s["qg"][:, a * qw + tc * 512:
                                      a * qw + tc * 512 + 512]
                        need = pje0 + len(kjobs) + a * (ROWS // 512) + tc + 1
                    else:
                        src = s["kg"][:, a * S + tc * 512:
                                      a * S + tc * 512 + 512]
                        need = pje0 + a * (S // 512) + tc + 1
                    scalar.wait_ge(pje.h, need)
                    if st >= 1:
                        scalar.wait_ge(smm.h, smm0 + st)
                    sqa.inc(scalar.activation(
                        s["sqs"][:, a * 512:(a + 1) * 512], src, AF.Square), 1)
            scalar.wait_ge(rmx.h, rmx0 + len(stat_steps) + 1)
            lns.inc(scalar.activation(s["mrow"][0:1, 2:3],
                                      s["mrow"][0:1, 4:5], AF.Sqrt), 1)

        @blk.vector
        def _(vector):
            for j, (kind, a, tc) in enumerate(jobs):
                par = (j % 2) * 512
                vector.wait_ge(pj.h, pj0 + 1 + j + 1)
                if kind == "k":
                    last = vector.tensor_scalar(
                        s["kg"][:, a * S + tc * 512:a * S + tc * 512 + 512],
                        pp[:, par:par + 512], s["cvcol"](2 * g + a, 1), None,
                        op0=ALU.add)
                elif kind == "q":
                    last = vector.tensor_scalar(
                        s["qg"][:, a * qw + tc * 512:
                                a * qw + tc * 512 + 512],
                        pp[:, par:par + 512], s["cvcol"](2 * g + a, 0), None,
                        op0=ALU.add)
                elif kind == "v":
                    t = a
                    vector.wait_ge(bve.h, bve0 + 1)
                    for h4 in range(GH):
                        last = vector.tensor_tensor(
                            s["vg"][:, t * 260 + h4 * 65:
                                    t * 260 + h4 * 65 + 64],
                            pp[:, par + h4 * 64:par + (h4 + 1) * 64],
                            s["bvb"][:, h4 * 64:(h4 + 1) * 64], ALU.add)
                else:
                    last = vector.tensor_scalar(
                        s["qg"][:, a * qw + 2048:a * qw + 2048 + 128],
                        pp[:, par:par + 128], s["cvcol"](2 * g + a, 0), None,
                        op0=ALU.add)
                pje.inc(last, 1)
            for st, (kind, tc) in enumerate(stat_steps):
                vector.wait_ge(smm.h, smm0 + st + 1)
                col = 0 if kind == "q" else 1
                vector.reduce_max(s["mrow"][0:1, 5:6], pn[:, :], axis=AX.X)
                if tc == 0:
                    ins = vector.tensor_copy(s["mrow"][0:1, col:col + 1],
                                             s["mrow"][0:1, 5:6])
                else:
                    ins = vector.tensor_tensor(s["mrow"][0:1, col:col + 1],
                                               s["mrow"][0:1, col:col + 1],
                                               s["mrow"][0:1, 5:6], ALU.max)
                rmx.inc(ins, 1)
            rmx.inc(vector.tensor_tensor(s["mrow"][0:1, 4:5],
                                         s["mrow"][0:1, 0:1],
                                         s["mrow"][0:1, 1:2], ALU.mult), 1)
            vector.wait_ge(lns.h, lns.n)
            nsel.inc(vector.tensor_scalar(s["mrow"][0:1, 3:4],
                                          s["mrow"][0:1, 2:3], -0.125, None,
                                          op0=ALU.mult), 1)
            vector.wait_ge(smm.h, smm0 + len(stat_steps) + 1)
            bze.inc(vector.tensor_copy(s["negm"][:, :], ps[:, :]), 1)

    qx = s["sqs"]
    with (
        nc.psum_tensor(f"g{g}_pS", [128, 2048], F32) as pS,
        nc.psum_tensor(f"g{g}_pPV", [65, 1024], F32) as pPV,
        nc.psum_tensor(f"g{g}_pB", [128, 1024], F32) as pB,
        nc.Block() as blk,
    ):
        NJ = S // 128
        iters = [(pair, ck) for pair in range(2) for ck in range(ROWS // 512)]
        bsm0, bex0, bpv0, lns0, bcm0, rbe0, bze0 = (
            bsm.n, bex.n, bpv.n, lns.n, bcm.n, rbe.n, bze.n)

        @blk.tensor
        def _(tensor):
            for it, (pair, ck) in enumerate(iters):
                a = pair
                ib = bsm0 + it * NJ
                eb = bex0 + it * NJ
                vb = bpv0 + it * NJ
                def emit_st(j):
                    sl_ = (j % 2) * 1024
                    for half in range(2):
                        rb_ = half * 64
                        ins = tensor.matmul(
                            pS[:, sl_ + half * 512:sl_ + half * 512 + 512],
                            s["kg"][rb_:rb_ + 64,
                                    a * S + j * 128:a * S + (j + 1) * 128],
                            s["qg"][rb_:rb_ + 64,
                                    a * qw + ck * 512:a * qw + ck * 512 + 512],
                            start=True, stop=True)
                    bsm.inc(ins, 1)

                if it > 0:
                    tensor.wait_ge(bze.h, bze0 + it)
                emit_st(0)
                emit_st(1)
                for j in range(NJ):
                    tensor.wait_ge(bex.h, eb + j + 1)
                    if j + 2 < NJ:
                        emit_st(j + 2)
                    slp = (j % 2) * 1024
                    for half in range(2):
                        ins = tensor.matmul(
                            pPV[:, half * 512:half * 512 + 512],
                            s["vg"][:, j * 260 + (2 * a + half) * 65:
                                    j * 260 + (2 * a + half) * 65 + 65],
                            s["pt"][:, slp + half * 512:
                                    slp + half * 512 + 512],
                            start=(j == 0), stop=(j == NJ - 1))
                    bpv.inc(ins, 1)
                tensor.wait_ge(lns.h, lns0 + it * 2 + 2)
                if it > 0:
                    tensor.wait_ge(rbe.h, rbe0 + it)
                for half in range(2):
                    ins = tensor.matmul(
                        pB[:, half * 512:half * 512 + 512],
                        s["ones65"][64:65, :],
                        s["lrec"][64:65, half * 512:half * 512 + 512],
                        start=True, stop=True)
                bcm.inc(ins, 1)

        @blk.scalar
        def _(scalar):
            for it, (h4, ic) in enumerate(iters):
                ib = bsm0 + it * NJ
                for j in range(NJ):
                    sl = (j % 2) * 1024
                    scalar.wait_ge(bsm.h, ib + j + 1)
                    if j >= 2:
                        scalar.wait_ge(bpv.h, bpv0 + it * NJ + j - 1)
                    bex.inc(scalar.activation(
                        s["pt"][:, sl:sl + 1024], pS[:, sl:sl + 1024],
                        AF.Exp, bias=s["negm"][:, :], scale=0.125), 1)
                scalar.wait_ge(bpv.h, bpv0 + (it + 1) * NJ)
                lns.inc(scalar.activation(s["rb"][64:65, 0:1024],
                                          pPV[64:65, :], AF.Ln), 1)
                scalar.wait_ge(lns.h, lns0 + it * 2 + 1)
                lns.inc(scalar.activation(s["lrec"][64:65, 0:1024],
                                          s["rb"][64:65, 0:1024],
                                          AF.Exp, scale=-1.0), 1)
                scalar.wait_ge(bcm.h, bcm0 + it + 1)
                if it > 0:
                    scalar.wait_ge(bze.h, bze0 + it)
                rbe.inc(scalar.copy(s["rb"][:, :], pB[:, :]), 1)

        @blk.vector
        def _(vector):
            if extra:
                vector.memset(qx[:, 0:512], 0.0)
                for h4 in range(GH):
                    a = h4 // 2
                    rb_ = 64 * (h4 % 2)
                    vector.tensor_copy(
                        qx[rb_:rb_ + 64, h4 * 128:(h4 + 1) * 128],
                        s["qg"][rb_:rb_ + 64,
                                a * qw + 2048:a * qw + 2048 + 128])
            for it, (pair, ck) in enumerate(iters):
                zp = 2 * g + pair
                vector.wait_ge(rbe.h, rbe0 + it + 1)
                for half in range(2):
                    zr = 64 * half
                    ins = vector.tensor_tensor(
                        s["z"][zr:zr + 64,
                               zp * zw + ck * 512:zp * zw + ck * 512 + 512],
                        pPV[0:64, half * 512:half * 512 + 512],
                        s["rb"][0:64, half * 512:half * 512 + 512], ALU.mult)
                bze.inc(ins, 1)
            if extra:
                vector.wait_ge(rbe.h, rbe0 + len(iters) + 1)
                last = None
                for h4 in range(GH):
                    h = g * GH + h4
                    zp, zr = h // 2, 64 * (h % 2)
                    last = vector.tensor_tensor(
                        s["z"][zr:zr + 64, zp * zw + 2048:zp * zw + 2176],
                        pPV[0:64, h4 * 128:(h4 + 1) * 128],
                        s["rb"][0:64, h4 * 128:(h4 + 1) * 128], ALU.mult)
                bze.inc(last, 1)

        if extra:
            NJx = S // 128

            @blk.tensor
            def _(tensor):
                tensor.wait_ge(bze.h, bze0 + len(iters))
                for j in range(NJx):
                    sl = (j % 2) * 1024
                    if j >= 2:
                        tensor.wait_ge(bex.h, bex0 + len(iters) * NJ + j - 1)
                    for h4 in range(GH):
                        a = h4 // 2
                        ins = tensor.matmul(
                            pS[:, sl + h4 * 128:sl + (h4 + 1) * 128],
                            s["kg"][:, a * S + j * 128:a * S + (j + 1) * 128],
                            qx[:, h4 * 128:(h4 + 1) * 128],
                            start=True, stop=True)
                    bsm.inc(ins, 1)
                    tensor.wait_ge(bex.h, bex0 + len(iters) * NJ + j + 1)
                    for h4 in range(GH):
                        ins = tensor.matmul(
                            pPV[:, h4 * 128:(h4 + 1) * 128],
                            s["vg"][:, j * 260 + h4 * 65:j * 260 + h4 * 65 + 65],
                            s["pt"][:, sl + h4 * 128:sl + (h4 + 1) * 128],
                            start=(j == 0 and h4 == 0),
                            stop=(j == NJx - 1 and h4 == GH - 1))
                    bpv.inc(ins, 1)
                tensor.wait_ge(lns.h, lns0 + len(iters) * 2 + 2 * GH)
                if True:
                    tensor.wait_ge(rbe.h, rbe0 + len(iters) + 0)
                for h4 in range(GH):
                    ins = tensor.matmul(
                        pB[:, h4 * 128:(h4 + 1) * 128],
                        s["ones65"][64:65, :],
                        s["lrec"][64:65, 512 + h4 * 128:512 + (h4 + 1) * 128],
                        start=True, stop=True)
                bcm.inc(ins, 1)

            @blk.scalar
            def _(scalar):
                eb = bex0 + len(iters) * NJ
                for j in range(NJx):
                    sl = (j % 2) * 1024
                    scalar.wait_ge(bsm.h, bsm0 + len(iters) * NJ + j + 1)
                    if j >= 2:
                        scalar.wait_ge(bpv.h, bpv0 + len(iters) * NJ + j - 1)
                    bex.inc(scalar.activation(
                        s["pt"][:, sl:sl + 512], pS[:, sl:sl + 512],
                        AF.Exp, bias=s["negm"][:, :], scale=0.125), 1)
                scalar.wait_ge(bpv.h, bpv0 + (len(iters) + 1) * NJ)
                for h4 in range(GH):
                    lns.inc(scalar.activation(
                        s["rb"][64:65, h4 * 128:(h4 + 1) * 128],
                        pPV[64:65, h4 * 128:(h4 + 1) * 128], AF.Ln), 1)
                    scalar.wait_ge(lns.h, lns.n)
                    lns.inc(scalar.activation(
                        s["lrec"][64:65, 512 + h4 * 128:512 + (h4 + 1) * 128],
                        s["rb"][64:65, h4 * 128:(h4 + 1) * 128],
                        AF.Exp, scale=-1.0), 1)
                scalar.wait_ge(bcm.h, bcm0 + len(iters) + 1)
                scalar.wait_ge(bze.h, bze0 + len(iters))
                rbe.inc(scalar.copy(s["rb"][:, 0:512], pB[:, 0:512]), 1)


def build_l1attn():
    nc = bass.Bass("TRN2", target_bir_lowering=False,
                   detect_race_conditions=False)
    xin = nc.dram_tensor("xin", [S, D], F32, kind="ExternalInput")
    wq2 = nc.dram_tensor("wq2", [D, D], F32, kind="ExternalInput")
    wk2 = nc.dram_tensor("wk2", [D, D], F32, kind="ExternalInput")
    wv2 = nc.dram_tensor("wv2", [D, D], F32, kind="ExternalInput")
    wo2 = nc.dram_tensor("wo2", [D, D], F32, kind="ExternalInput")
    cvecs = nc.dram_tensor("cvecs", [5, 1024], F32, kind="ExternalInput")
    bvrow = nc.dram_tensor("bvrow", [1, 1024], F32, kind="ExternalInput")
    ident = nc.dram_tensor("ident", [128, 128], F32, kind="ExternalInput")
    xout = nc.dram_tensor("xout", [ROWS, D], F32, kind="ExternalOutput")

    with contextlib.ExitStack() as es:
        P = SemPool(nc, es)
        E = es.enter_context
        s_ident = E(nc.sbuf_tensor("s_ident", [128, 128], F32))
        s_ones_c = E(nc.sbuf_tensor("ones_c", [128, 1], BF16))
        s_ones_r = E(nc.sbuf_tensor("ones_r", [1, 128], F32))
        s_ones65 = E(nc.sbuf_tensor("ones65", [65, 128], F32))
        s_cpool = E(nc.sbuf_tensor("cpool", [128, 4], F32))
        s_cvc = E(nc.sbuf_tensor("cvc", [128, DP * 5], F32))
        s_bvr = E(nc.sbuf_tensor("bvr", [1, 1024], F32))

        emit_consts(nc, P, s_ident, ident, s_ones_c, s_ones_r,
                    s_cpool, [1e-5, 1e-3])
        eps5 = s_cpool[0:1, 0:1]
        eps3 = s_cpool[0:1, 1:2]
        emit_cvecs(nc, P, "cv", cvecs, s_cvc, 5, 5, s_ident)

        def cvcol(dp, j):
            return s_cvc[:, dp * 5 + j:dp * 5 + j + 1]

        aux = P.take(["aux_a"])[0]
        with nc.Block() as blk:
            @blk.sync
            def _(sync):
                aux.inc(sync.dma_start(out=s_bvr[:, :], in_=bvrow[:, :]), 16)
                sync.wait_ge(aux.h, aux.n)

            @blk.gpsimd
            def _(gp):
                gp.memset(s_ones65[:, :], 1.0)

        es_r = contextlib.ExitStack()
        s_z = es_r.enter_context(
            nc.sbuf_tensor("zbuf", [128, DP * ROWS], BF16, side="right"))
        s_xln = es_r.enter_context(
            nc.sbuf_tensor("xln", [128, DP * S], BF16, side="right"))

        with (
            nc.sbuf_tensor("wq", [128, DP * 1024], BF16) as s_wq,
            nc.sbuf_tensor("wk", [128, DP * 1024], BF16) as s_wk,
            nc.sbuf_tensor("wv", [128, DP * 1024], BF16) as s_wv,
        ):
            wl = P.take(["wl_w"])[0]
            with nc.Block() as blk:
                @blk.gpsimd
                def _(gp):
                    emit_wload(nc, gp, wq2, s_wq, wl)
                    emit_wload(nc, gp, wk2, s_wk, wl)
                    emit_wload(nc, gp, wv2, s_wv, wl)
                    gp.wait_ge(wl.h, wl.n)
            emit_load_cm(nc, P, "ld", xin, S // 128, s_xln, s_ident)
            emit_ln_cm(nc, P, "ln1", s_xln, S, eps5,
                       s_ones_c, s_ones_r, dst=s_xln)

            with (
                nc.sbuf_tensor("kg", [128, 2 * S], BF16) as s_kg,
                nc.sbuf_tensor("qg", [128, 2 * ROWS], BF16) as s_qg,
                nc.sbuf_tensor("vg", [128, 32 * 260], BF16) as s_vg,
                nc.sbuf_tensor("pt", [128, 2 * 1024], BF16) as s_pt,
                nc.sbuf_tensor("sqs", [128, 1024], BF16) as s_sqs,
                nc.sbuf_tensor("mrow", [1, 8], F32) as s_mrow,
                nc.sbuf_tensor("negm", [128, 1], F32) as s_negm,
                nc.sbuf_tensor("bvb", [128, 256], F32) as s_bvb,
                nc.sbuf_tensor("lrec", [65, 1024], F32) as s_lrec,
                nc.sbuf_tensor("rb", [128, 1024], F32) as s_rb,
            ):
                with nc.Block() as blk:
                    @blk.gpsimd
                    def _(gp):
                        vones = s_vg.ap().rearrange(
                            "p (t h c) -> p t h c", h=GH, c=65)
                        gp.memset(vones[:, :, :, 64:65], 1.0)

                sems = P.take(["at_pj", "at_pje", "at_bve", "at_sqa",
                               "at_smm", "at_rmx", "at_nsel", "at_bsm",
                               "at_bex", "at_bpv", "at_lns", "at_bcm",
                               "at_rbe", "at_bze"])
                bufs = {"xln": s_xln, "wq": s_wq, "wk": s_wk,
                        "wv": s_wv, "kg": s_kg, "qg": s_qg,
                        "vg": s_vg, "pt": s_pt, "sqs": s_sqs,
                        "mrow": s_mrow, "negm": s_negm, "bvb": s_bvb,
                        "lrec": s_lrec, "rb": s_rb, "z": s_z,
                        "ones_c": s_ones_c, "ones_r": s_ones_r,
                        "ones65": s_ones65, "bvr": s_bvr,
                        "cvcol": cvcol}
                for g in range(NG):
                    _emit_l1attn_group(nc, es, g, bufs, sems)

        with nc.sbuf_tensor("ybuf", [128, DP * ROWS], BF16) as s_y:
            with nc.sbuf_tensor("wo", [128, DP * 1024], BF16) as s_wo:
                wl2 = P.take(["wl2_w"])[0]
                with nc.Block() as blk:
                    @blk.gpsimd
                    def _(gp):
                        emit_wload(nc, gp, wo2, s_wo, wl2)
                        gp.wait_ge(wl2.h, wl2.n)

                wsem, wsc, wev = P.take(["wo_m", "wo_s", "wo_e"])
                wm0, ws0, we0 = wsem.n, wsc.n, wev.n
                with (
                    nc.sbuf_tensor("wot", [128, 1024], F32) as s_wot,
                    nc.psum_tensor("wo_pd", [128, 1024], F32) as pd,
                    nc.Block() as blk,
                ):
                    njobs = DP * (ROWS // 512)

                    @blk.tensor
                    def _(tensor):
                        nj = 0
                        for dpo in range(DP):
                            for tc in range(ROWS // 512):
                                par = (nj % 2) * 512
                                if nj >= 2:
                                    tensor.wait_ge(wsc.h, ws0 + nj - 1)
                                for hkp in range(DP):
                                    ins = tensor.matmul(
                                        pd[:, par:par + 512],
                                        s_wo[:, hkp * 1024 + dpo * 128:
                                             hkp * 1024 + (dpo + 1) * 128],
                                        s_z[:, hkp * ROWS + tc * 512:
                                            hkp * ROWS + tc * 512 + 512],
                                        start=(hkp == 0),
                                        stop=(hkp == DP - 1))
                                wsem.inc(ins, 1)
                                nj += 1

                    @blk.scalar
                    def _(scalar):
                        for nj in range(njobs):
                            par = (nj % 2) * 512
                            scalar.wait_ge(wsem.h, wm0 + nj + 1)
                            if nj >= 2:
                                scalar.wait_ge(wev.h, we0 + nj - 1)
                            dpo = nj // (ROWS // 512)
                            wsc.inc(scalar.activation(
                                s_wot[:, par:par + 512],
                                pd[:, par:par + 512],
                                AF.Identity, bias=cvcol(dpo, 2)), 1)

                    @blk.vector
                    def _(vector):
                        for nj in range(njobs):
                            par = (nj % 2) * 512
                            dpo = nj // (ROWS // 512)
                            tc = nj % (ROWS // 512)
                            vector.wait_ge(wsc.h, ws0 + nj + 1)
                            wev.inc(vector.tensor_tensor(
                                s_y[:, dpo * ROWS + tc * 512:
                                    dpo * ROWS + tc * 512 + 512],
                                s_wot[:, par:par + 512],
                                s_xln[:, dpo * S + tc * 512:
                                      dpo * S + tc * 512 + 512],
                                ALU.add), 1)

            es_r.close()

            gcols = [cvcol(dp, 3) for dp in range(DP)]
            bcols = [cvcol(dp, 4) for dp in range(DP)]
            emit_ln_cm(nc, P, "lni", s_y, ROWS, eps3, s_ones_c, s_ones_r,
                       dst=s_y, gamma_cols=gcols, beta_cols=bcols)

            with nc.sbuf_tensor("l2f", [128, DP * ROWS], F32) as s_l2f:
                emit_ln_cm(nc, P, "ln2", s_y, ROWS, eps5,
                           s_ones_c, s_ones_r, dst=s_l2f)
                with nc.sbuf_tensor("xtm2", [128, (ROWS // 128) * 1024],
                                    F32) as s_xtm2:
                    emit_reload_tm(nc, P, "rl", xin, s_xtm2, ROWS // 128)
                    emit_store_tm(nc, P, "st", s_l2f, xout, ROWS // 128,
                                  s_ident, add_tm=s_xtm2)
    return nc


def l1attn_inmaps(Xstate, Wq, bq, Wk, bk, Wv, bv, Wo, bo, gamma, beta):
    wq2 = np.ascontiguousarray(Wq.reshape(D, D))
    wk2 = np.ascontiguousarray(Wk.reshape(D, D))
    wv2 = np.ascontiguousarray(Wv.reshape(D, D))
    wo2 = np.ascontiguousarray(Wo.reshape(D, D))
    cv = np.stack([bq.reshape(-1), bk.reshape(-1), bo,
                   gamma, beta]).astype(np.float32)
    bvr = bv.reshape(1, D).astype(np.float32)
    maps = []
    for c in range(NCORE):
        b, h = c // 2, c % 2
        xrot = np.concatenate([Xstate[b, h * ROWS:(h + 1) * ROWS],
                               Xstate[b, (1 - h) * ROWS:(2 - h) * ROWS]])
        maps.append({"xin": np.ascontiguousarray(xrot), "wq2": wq2,
                     "wk2": wk2, "wv2": wv2, "wo2": wo2, "cvecs": cv,
                     "bvrow": bvr, "ident": _IDENT})
    return maps


def run_l1attn(Xstate, *args):
    nc = build_l1attn()
    res = run_bass_kernel_spmd(nc, l1attn_inmaps(Xstate, *args),
                               core_ids=list(range(NCORE)))
    out = np.empty_like(Xstate)
    for c in range(NCORE):
        b, h = c // 2, c % 2
        out[b, h * ROWS:(h + 1) * ROWS] = res.results[c]["xout"]
    return out


def emit_store_stream(nc, P, tag, src_cm_bf, add_dram, dst_dram, nt,
                      ident_f32):
    ntok = src_cm_bf.shape[1] // DP
    lds = P.take(["ss_l0", "ss_l1", "ss_l2", "ss_l3"])
    tr, ev, ib = P.take(["ss_tr", "ss_ev", "ss_ib"])
    st = P.take(["ss_s0", "ss_s1"])
    lb = [s.n for s in lds]
    tr0, ev0, ib0 = tr.n, ev.n, ib.n
    sb = [s.n for s in st]
    with (
        nc.sbuf_tensor(f"{tag}_identb", [128, 128], BF16) as s_idb,
        nc.sbuf_tensor(f"{tag}_xa", [128, 4 * 1024], F32) as s_xa,
        nc.sbuf_tensor(f"{tag}_o", [128, 2 * 1024], F32) as s_out,
        nc.psum_tensor(f"{tag}_pA", [128, 512], BF16) as pA,
        nc.psum_tensor(f"{tag}_pB", [128, 512], BF16) as pB,
        nc.Block() as blk,
    ):
        @blk.vector
        def _(vector):
            ib.inc(vector.tensor_copy(s_idb[:, :], ident_f32[:, :]), 1)
            for n in range(nt):
                ob = (n % 2) * 1024
                if n >= 2:
                    vector.wait_ge(st[n % 2].h,
                                   sb[n % 2] + 16 * ((n - 2) // 2 + 1))
                vector.wait_ge(lds[n % 4].h, lb[n % 4] + 16 * (n // 4 + 1))
                for half in range(2):
                    ps = pA if half == 0 else pB
                    vector.wait_ge(tr.h, tr0 + 2 * n + half + 1)
                    last = None
                    for q in range(4):
                        dp = half * 4 + q
                        last = vector.tensor_tensor(
                            s_out[:, ob + dp * 128:ob + (dp + 1) * 128],
                            s_xa[:, (n % 4) * 1024 + dp * 128:
                                 (n % 4) * 1024 + (dp + 1) * 128],
                            ps[:, q * 128:(q + 1) * 128], ALU.add)
                    ev.inc(last, 1)

        @blk.tensor
        def _(tensor):
            tensor.wait_ge(ib.h, ib0 + 1)
            for n in range(nt):
                for half in range(2):
                    ps = pA if half == 0 else pB
                    if n > 0:
                        tensor.wait_ge(ev.h, ev0 + 2 * (n - 1) + half + 1)
                    for q in range(4):
                        dp = half * 4 + q
                        ins = tensor.transpose(
                            ps[:, q * 128:(q + 1) * 128],
                            src_cm_bf[:, dp * ntok + n * 128:
                                      dp * ntok + (n + 1) * 128],
                            s_idb[:, :])
                        if q == 3:
                            tr.inc(ins, 1)

        @blk.sync
        def _(sync):
            for n in range(nt):
                sl = n % 4
                if n >= 4:
                    sync.wait_ge(ev.h, ev0 + 2 * (n - 4) + 2)
                lds[sl].inc(sync.dma_start(
                    out=s_xa[:, sl * 1024:(sl + 1) * 1024],
                    in_=add_dram[n * 128:(n + 1) * 128, :]), 16)

        @blk.gpsimd
        def _(gp):
            for n in range(nt):
                gp.wait_ge(ev.h, ev0 + 2 * (n + 1))
                ob = (n % 2) * 1024
                st[n % 2].inc(gp.dma_start(
                    out=dst_dram[n * 128:(n + 1) * 128, :],
                    in_=s_out[:, ob:ob + 1024]), 16)
            gp.wait_ge(st[0].h, st[0].n)
            gp.wait_ge(st[1].h, st[1].n)


def _emit_l0attn_head(nc, es, hh, b, sems):
    pj, pje, aqk, adt, act_, acv, est, est2, sde, sdf = sems
    sdsl = [sde, sdf]
    xln, qh, kh, vh = b["xln"], b["qh"], b["kh"], b["vh"]

    qjobs = [("q", a, tc) for a in range(2) for tc in range(S // 512)]
    kjobs = [("k", a, tc) for a in range(2) for tc in range(S // 512)]
    vjobs = [("v", tt, a) for tt in range(32) for a in range(2)]
    jobs = qjobs + kjobs + vjobs
    pj0, pje0 = pj.n, pje.n
    with (
        nc.psum_tensor(f"h{hh}_pp", [128, 1024], F32) as pp,
        nc.Block() as blk,
    ):
        @blk.tensor
        def _(tensor):
            for j, (kind, a1, a2) in enumerate(jobs):
                par = (j % 2) * 512
                if j >= 2:
                    tensor.wait_ge(pje.h, pje0 + j - 1)
                t = 2 * hh + (a1 if kind != "v" else a2)
                if kind == "q":
                    ins = tensor.matmul(
                        pp[:, par:par + 512],
                        b["wqp"][:, t * 128:(t + 1) * 128],
                        xln[:, t * S + a2 * 512:t * S + a2 * 512 + 512],
                        start=True, stop=True)
                elif kind == "k":
                    ins = tensor.matmul(
                        pp[:, par:par + 512],
                        b["wkp"][:, t * 128:(t + 1) * 128],
                        xln[:, t * S + a2 * 512:t * S + a2 * 512 + 512],
                        start=True, stop=True)
                else:
                    tt_ = a1
                    ins = tensor.matmul(
                        pp[:, par:par + 128],
                        xln[:, t * S + tt_ * 128:t * S + (tt_ + 1) * 128],
                        b["wvp"][:, t * 128:(t + 1) * 128],
                        start=True, stop=True)
                pj.inc(ins, 1)

        @blk.vector
        def _(vector):
            for j, (kind, a1, a2) in enumerate(jobs):
                par = (j % 2) * 512
                vector.wait_ge(pj.h, pj0 + j + 1)
                if kind == "q":
                    ins = vector.tensor_copy(
                        qh[:, a1 * S + a2 * 512:a1 * S + a2 * 512 + 512],
                        pp[:, par:par + 512])
                elif kind == "k":
                    ins = vector.tensor_copy(
                        kh[:, a1 * S + a2 * 512:a1 * S + a2 * 512 + 512],
                        pp[:, par:par + 512])
                else:
                    tt_, a = a1, a2
                    ins = vector.tensor_copy(
                        vh[:, tt_ * 257 + a * 128:tt_ * 257 + (a + 1) * 128],
                        pp[:, par:par + 128])
                pje.inc(ins, 1)

    with (
        nc.psum_tensor(f"h{hh}_pq", [128, 1024], F32) as pq,
        nc.psum_tensor(f"h{hh}_po", [128, 1024], F32) as po,
        nc.psum_tensor(f"h{hh}_pr", [4, 512], F32) as pr,
        nc.psum_tensor(f"h{hh}_pb", [128, 512], F32) as pb,
        nc.psum_tensor(f"h{hh}_pe", [128, 1024], F32) as pe,
        nc.Block() as blk,
    ):
        njt = [4 * ic + 4 for ic in range(8)]
        qk0, dt0, ct0, cv0 = aqk.n, adt.n, act_.n, acv.n
        jtbase = [sum(njt[:ic]) for ic in range(8)]
        NCH = 28
        cs_ = [est, est2]
        b0_ = [est.n, est2.n]
        epi0 = ((b0_[0] + b0_[1]) // NCH)

        def chain(ic):
            return b0_[ic % 2] + (ic // 2) * NCH

        def csem(ic):
            return cs_[ic % 2]

        def prevgate(ic):
            if ic >= 1:
                return (cs_[(ic - 1) % 2],
                        b0_[(ic - 1) % 2] + ((ic - 1) // 2) * NCH)
            if b0_[1] >= NCH:
                return cs_[1], b0_[1] - NCH
            return None, 0

        @blk.tensor
        def _(tensor):
            for ic in range(8):
                c0 = chain(ic)
                cse = csem(ic)
                pgs, pg0 = prevgate(ic)
                if pgs is not None:
                    tensor.wait_ge(pgs.h, pg0 + 11)
                tensor.matmul(pb[:, :], b["sel4"][:, hh * 128:(hh + 1) * 128],
                              b["nrm4"][0:4, ic * 512:ic * 512 + 512],
                              start=True, stop=True).then_inc(cse.h, 1)
                def emit_qk(jt):
                    par_ = (jt % 2) * 512
                    if jt >= 2:
                        tensor.wait_ge(act_.h, ct0 + jtbase[ic] + jt - 1)
                    for a in range(2):
                        ins = tensor.matmul(
                            pq[:, par_:par_ + 512],
                            kh[:, a * S + jt * 128:a * S + (jt + 1) * 128],
                            qh[:, a * S + ic * 512:a * S + ic * 512 + 512],
                            start=(a == 0), stop=(a == 1))
                    aqk.inc(ins, 1)

                emit_qk(0)
                for jt in range(njt[ic]):
                    par = (jt % 2) * 512
                    if jt + 1 < njt[ic]:
                        emit_qk(jt + 1)
                    tensor.wait_ge(act_.h, ct0 + jtbase[ic] + jt + 1)
                    tensor.matmul(po[:, 0:512],
                                  vh[:, jt * 257:jt * 257 + 128],
                                  b["ct"][:, par:par + 512],
                                  start=(jt == 0), stop=(jt == njt[ic] - 1))
                    tensor.matmul(po[:, 512:1024],
                                  vh[:, jt * 257 + 128:jt * 257 + 256],
                                  b["ct"][:, par:par + 512],
                                  start=(jt == 0), stop=(jt == njt[ic] - 1))
                    ins = tensor.matmul(
                        pr[:, :], b["oc4"][:, hh * 4:(hh + 1) * 4],
                        b["ct"][:, par:par + 512],
                        start=(jt == 0), stop=(jt == njt[ic] - 1))
                    acv.inc(ins, 1)
                tensor.wait_ge(cse.h, c0 + 7)
                tensor.matmul(pb[:, :], b["sel4"][:, hh * 128:(hh + 1) * 128],
                              b["rec"][0:4, 512:1024],
                              start=True, stop=True).then_inc(cse.h, 1)
                if pgs is not None:
                    tensor.wait_ge(pgs.h, pg0 + NCH)
                tensor.wait_ge(cse.h, c0 + 11)
                for a in range(2):
                    ins = tensor.matmul(pe[0:1, 0:512], b["ones_cf"][:, :],
                                        b["ht"][:, a * 512:a * 512 + 512],
                                        start=(a == 0), stop=(a == 1))
                ins.then_inc(cse.h, 1)
                tensor.wait_ge(cse.h, c0 + 14)
                for a in range(2):
                    ins = tensor.matmul(pe[0:1, 512:1024], b["ones_c"][:, :],
                                        b["sqe"][:, a * 512:a * 512 + 512],
                                        start=(a == 0), stop=(a == 1))
                ins.then_inc(cse.h, 1)
                tensor.wait_ge(cse.h, c0 + 20)
                tensor.matmul(pe[:, 0:512], b["ones_r"][:, :],
                              b["erow"][0:1, 0:512],
                              start=True, stop=True).then_inc(cse.h, 1)
                tensor.matmul(pe[:, 512:1024], b["ones_r"][:, :],
                              b["erow"][0:1, 512:1024],
                              start=True, stop=True).then_inc(cse.h, 1)
        aqk.n = qk0 + sum(njt)
        acv.n = cv0 + sum(njt)

        @blk.scalar
        def _(scalar):
            for ic in range(8):
                c0 = chain(ic)
                cse = csem(ic)
                scalar.wait_ge(cse.h, c0 + 1)
                scalar.copy(b["bi"][:, :], pb[:, :]).then_inc(cse.h, 1)
                for jt in range(njt[ic]):
                    par = (jt % 2) * 512
                    if jt >= 2:
                        scalar.wait_ge(act_.h, ct0 + jtbase[ic] + jt - 1)
                    adt.inc(scalar.activation(
                        b["dt"][:, par:par + 512], b["bi"][:, :], AF.Exp,
                        bias=b["gcols"][:, jt * 4 + hh:jt * 4 + hh + 1]), 1)
                scalar.wait_ge(cse.h, c0 + 2)
                scalar.wait_ge(acv.h, cv0 + (ic + 1) * 0 + jtbase[ic] + njt[ic])
                scalar.activation(b["nr"][:, :], pr[:, :],
                                  AF.Abs).then_inc(cse.h, 1)
                scalar.wait_ge(cse.h, c0 + 5)
                scalar.activation(b["rec"][0:4, 0:512], b["nr"][:, :],
                                  AF.Ln).then_inc(cse.h, 1)
                scalar.wait_ge(cse.h, c0 + 6)
                scalar.activation(b["rec"][0:4, 512:1024],
                                  b["rec"][0:4, 0:512], AF.Exp,
                                  scale=-1.0).then_inc(cse.h, 1)
                scalar.wait_ge(cse.h, c0 + 8)
                scalar.copy(b["rbf"][:, :], pb[:, :]).then_inc(cse.h, 1)
                scalar.wait_ge(cse.h, c0 + 13)
                scalar.activation(b["sqe"][:, :], b["ht"][:, :],
                                  AF.Square).then_inc(cse.h, 1)
                scalar.wait_ge(cse.h, c0 + 18)
                scalar.activation(b["nr"][0:1, :],
                                  b["erow"][0:1, 512:1024], AF.Ln,
                                  bias=b["cpool"][0:1, 0:1]).then_inc(cse.h, 1)
                scalar.wait_ge(cse.h, c0 + 19)
                scalar.activation(b["erow"][0:1, 512:1024], b["nr"][0:1, :],
                                  AF.Exp, scale=-0.5).then_inc(cse.h, 1)
        adt.n = dt0 + sum(njt)

        @blk.vector
        def _(vector):
            for ic in range(8):
                c0 = chain(ic)
                cse = csem(ic)
                epi = epi0 + ic
                for jt in range(njt[ic]):
                    par = (jt % 2) * 512
                    vector.wait_ge(adt.h, dt0 + jtbase[ic] + jt + 1)
                    vector.wait_ge(aqk.h, qk0 + jtbase[ic] + jt + 1)
                    if jt >= 2:
                        vector.wait_ge(acv.h, cv0 + jtbase[ic] + jt - 1)
                    if jt >= 4 * ic:
                        off = (jt - 4 * ic) * 128
                        if off > 0:
                            vector.memset(b["dt"][:, par:par + off], 0.0)
                        vector.tensor_tensor(
                            b["dt"][:, par + off:par + off + 128],
                            b["dt"][:, par + off:par + off + 128],
                            b["tril"][:, :], ALU.mult)
                    act_.inc(vector.scalar_tensor_tensor(
                        b["ct"][:, par:par + 512], pq[:, par:par + 512],
                        0.0625, b["dt"][:, par:par + 512],
                        op0=ALU.mult, op1=ALU.mult), 1)
                c = b
                vector.wait_ge(cse.h, c0 + 3)
                vector.tensor_tensor(c["nr"][:, :], c["nr"][:, :],
                                     c["em4"][0:4, ic * 512:ic * 512 + 512],
                                     ALU.max).then_inc(cse.h, 1)
                vector.wait_ge(cse.h, c0 + 4)
                vector.tensor_scalar(c["nr"][:, :], c["nr"][:, :], 1e-8,
                                     None, op0=ALU.add).then_inc(cse.h, 1)
                vector.wait_ge(cse.h, c0 + 9)
                vector.tensor_tensor(c["ht"][:, 0:512], po[:, 0:512],
                                     c["rbf"][:, :],
                                     ALU.mult).then_inc(cse.h, 1)
                vector.wait_ge(cse.h, c0 + 10)
                vector.tensor_tensor(c["ht"][:, 512:1024], po[:, 512:1024],
                                     c["rbf"][:, :],
                                     ALU.mult).then_inc(cse.h, 1)
                vector.wait_ge(cse.h, c0 + 12)
                vector.tensor_scalar(c["erow"][0:1, 0:512], pe[0:1, 0:512],
                                     1.0 / 256.0, None,
                                     op0=ALU.mult).then_inc(cse.h, 1)
                vector.wait_ge(cse.h, c0 + 15)
                vector.tensor_scalar(c["erow"][0:1, 512:1024],
                                     pe[0:1, 512:1024], 1.0 / 256.0, None,
                                     op0=ALU.mult).then_inc(cse.h, 1)
                vector.wait_ge(cse.h, c0 + 16)
                vector.tensor_tensor(c["nr"][0:1, :], c["erow"][0:1, 0:512],
                                     c["erow"][0:1, 0:512],
                                     ALU.mult).then_inc(cse.h, 1)
                vector.wait_ge(cse.h, c0 + 17)
                vector.tensor_tensor(c["erow"][0:1, 512:1024],
                                     c["erow"][0:1, 512:1024],
                                     c["nr"][0:1, :],
                                     ALU.subtract).then_inc(cse.h, 1)
                sl = (epi % 2) * 1024
                if epi >= 2:
                    vector.wait_ge(sdsl[epi % 2].h,
                                   32 * ((epi - 2) // 2 + 1))
                for a in range(2):
                    vector.wait_ge(cse.h, c0 + 22 + a * 3)
                    vector.tensor_tensor(c["ht"][:, a * 512:a * 512 + 512],
                                         c["ht"][:, a * 512:a * 512 + 512],
                                         pe[:, 0:512],
                                         ALU.subtract).then_inc(cse.h, 1)
                    vector.tensor_tensor(c["ht"][:, a * 512:a * 512 + 512],
                                         c["ht"][:, a * 512:a * 512 + 512],
                                         pe[:, 512:1024],
                                         ALU.mult).then_inc(cse.h, 1)
                    vector.tensor_scalar(
                        c["so2"][:, sl + a * 512:sl + a * 512 + 512],
                        c["ht"][:, a * 512:a * 512 + 512],
                        c["gam"][:, 2 * hh + a:2 * hh + a + 1], None,
                        op0=ALU.mult).then_inc(cse.h, 1)
        act_.n = ct0 + sum(njt)
        est.n = b0_[0] + 4 * NCH
        est2.n = b0_[1] + 4 * NCH

        @blk.sync
        def _(sync):
            for ic in range(8):
                c0 = chain(ic)
                cse = csem(ic)
                epi = epi0 + ic
                sl = (epi % 2) * 1024
                sync.wait_ge(cse.h, c0 + 28)
                for a in range(2):
                    sdsl[epi % 2].inc(sync.dma_start(
                        out=b["sout"][(2 * hh + a) * 128:
                                      (2 * hh + a + 1) * 128,
                                      ic * 512:(ic + 1) * 512],
                        in_=b["so2"][:, sl + a * 512:sl + a * 512 + 512]), 16)


NH0 = 4
L0DEBUG = False
DH0 = 256


def build_l0attn():
    nc = bass.Bass("TRN2", target_bir_lowering=False,
                   detect_race_conditions=False)
    xin = nc.dram_tensor("xin", [S, D], F32, kind="ExternalInput")
    wqp = nc.dram_tensor("wqp", [D, 128], F32, kind="ExternalInput")
    wkp = nc.dram_tensor("wkp", [D, 128], F32, kind="ExternalInput")
    wvp = nc.dram_tensor("wvp", [D, 128], F32, kind="ExternalInput")
    wi_d = nc.dram_tensor("wi_d", [4, HID], F32, kind="ExternalInput")
    wf_d = nc.dram_tensor("wf_d", [4, HID], F32, kind="ExternalInput")
    bi_d = nc.dram_tensor("bi_d", [4, 1], F32, kind="ExternalInput")
    bf_d = nc.dram_tensor("bf_d", [4, 1], F32, kind="ExternalInput")
    cvecs = nc.dram_tensor("cvecs", [1, 1024], F32, kind="ExternalInput")
    tril = nc.dram_tensor("tril", [128, 128], F32, kind="ExternalInput")
    sel4 = nc.dram_tensor("sel4", [4, 512], F32, kind="ExternalInput")
    oc4 = nc.dram_tensor("oc4", [128, 16], F32, kind="ExternalInput")
    ident = nc.dram_tensor("ident", [128, 128], F32, kind="ExternalInput")
    xout = nc.dram_tensor("xout", [S, D], F32, kind="ExternalOutput")
    if L0DEBUG:
        drows = nc.dram_tensor("drows", [12, S], F32, kind="ExternalOutput")
        drows2 = nc.dram_tensor("drows2", [8, S], F32, kind="ExternalOutput")
        dwg = nc.dram_tensor("dwg", [128, DP * 8], F32, kind="ExternalOutput")
        dsout = nc.dram_tensor("dsout", [D, S], F32, kind="ExternalOutput")
    sout = nc.dram_tensor("sout_scr", [D, S], BF16)

    with contextlib.ExitStack() as es:
        P = SemPool(nc, es)
        E = es.enter_context
        s_ident = E(nc.sbuf_tensor("s_ident", [128, 128], F32))
        s_ones_c = E(nc.sbuf_tensor("ones_c", [128, 1], BF16))
        s_ones_cf = E(nc.sbuf_tensor("ones_cf", [128, 1], F32))
        s_ones_r = E(nc.sbuf_tensor("ones_r", [1, 128], F32))
        s_cpool = E(nc.sbuf_tensor("cpool", [128, 4], F32))
        s_cvc = E(nc.sbuf_tensor("cvc", [128, DP * 1], F32))
        s_gam = E(nc.sbuf_tensor("gam", [128, DP], F32))
        s_tril = E(nc.sbuf_tensor("s_tril", [128, 128], F32))
        s_sel4 = E(nc.sbuf_tensor("s_sel4", [4, 512], F32))
        s_oc4 = E(nc.sbuf_tensor("s_oc4", [128, 16], BF16))
        s_wqp = E(nc.sbuf_tensor("s_wqp", [128, 8 * 128], BF16))
        s_wkp = E(nc.sbuf_tensor("s_wkp", [128, 8 * 128], BF16))
        s_wvp = E(nc.sbuf_tensor("s_wvp", [128, 8 * 128], BF16))
        s_wg = E(nc.sbuf_tensor("wg", [128, DP * 8], BF16))
        s_bi4 = E(nc.sbuf_tensor("bi4", [4, 1], F32))
        s_bf4 = E(nc.sbuf_tensor("bf4", [4, 1], F32))
        s_rA = E(nc.sbuf_tensor("rA", [4, S], F32))
        s_rD = E(nc.sbuf_tensor("rD", [4, S], F32))
        s_rE = E(nc.sbuf_tensor("rE", [4, S], F32))
        s_gcols = E(nc.sbuf_tensor("gcols", [128, 32 * 4], F32))

        emit_consts(nc, P, s_ident, ident, s_ones_c, s_ones_r,
                    s_cpool, [1e-5, 1e-3, 1.0, 1e-8])
        eps5 = s_cpool[0:1, 0:1]
        one_c4 = s_cpool[0:4, 2:3]
        emit_cvecs(nc, P, "cv", cvecs, s_cvc, 1, 1, s_ident)

        aux = P.take(["aux_a", "aux_b"])
        auxs, auxg = aux
        with nc.Block() as blk:
            @blk.sync
            def _(sync):
                auxs.inc(sync.dma_start(out=s_tril[:, :], in_=tril[:, :]), 16)
                auxs.inc(sync.dma_start(out=s_sel4[:, :], in_=sel4[:, :]), 16)
                auxs.inc(sync.dma_start(out=s_bi4[:, :], in_=bi_d[:, :]), 16)
                auxs.inc(sync.dma_start(out=s_bf4[:, :], in_=bf_d[:, :]), 16)
                sync.wait_ge(auxs.h, auxs.n)

            @blk.gpsimd
            def _(gp):
                auxg.inc(gp.dma_start(out=s_oc4[:, :], in_=oc4[:, :]), 16)
                auxg.inc(gp.dma_start(
                    out=s_wqp.ap().rearrange("p (a c) -> p a c", c=128),
                    in_=wqp.ap().rearrange("(a p) c -> p a c", p=128)), 16)
                auxg.inc(gp.dma_start(
                    out=s_wkp.ap().rearrange("p (a c) -> p a c", c=128),
                    in_=wkp.ap().rearrange("(a p) c -> p a c", p=128)), 16)
                auxg.inc(gp.dma_start(
                    out=s_wvp.ap().rearrange("p (a c) -> p a c", c=128),
                    in_=wvp.ap().rearrange("(a p) c -> p a c", p=128)), 16)
                gp.memset(s_ones_cf[:, :], 1.0)
                gp.wait_ge(auxg.h, auxg.n)

            @blk.vector
            def _(vector):
                for dp in range(DP):
                    vector.tensor_scalar(s_gam[:, dp:dp + 1],
                                         s_cvc[:, dp:dp + 1], 1.0, None,
                                         op0=ALU.add)

        gws = P.take(["gw_l", "gw_t", "gw_e"])
        gwl, gwt, gwe = gws
        with (
            nc.sbuf_tensor("wirow", [4, HID], F32) as s_wir,
            nc.sbuf_tensor("wfrow", [4, HID], F32) as s_wfr,
            nc.psum_tensor("gw_pc", [128, 8], F32) as pc,
            nc.Block() as blk,
        ):
            @blk.sync
            def _(sync):
                gwl.inc(sync.dma_start(out=s_wir[:, :], in_=wi_d[:, :]), 16)
                gwl.inc(sync.dma_start(out=s_wfr[:, :], in_=wf_d[:, :]), 16)

            @blk.vector
            def _(vector):
                vector.wait_ge(gwl.h, 32)
                for rows in (s_wir, s_wfr):
                    vector.tensor_tensor(rows[:, 0:1024], rows[:, 0:1024],
                                         rows[:, 1024:2048], ALU.add)
                    gwe.inc(vector.tensor_tensor(rows[:, 0:1024],
                                                 rows[:, 0:1024],
                                                 rows[:, 2048:3072], ALU.add),
                            1)
                for dp in range(DP):
                    vector.wait_ge(gwt.h, dp + 1)
                    gwe.inc(vector.tensor_copy(
                        s_wg[:, dp * 8:(dp + 1) * 8], pc[:, 0:8]), 1)

            @blk.tensor
            def _(tensor):
                tensor.wait_ge(gwe.h, 2)
                for dp in range(DP):
                    if dp >= 1:
                        tensor.wait_ge(gwe.h, 2 + dp)
                    tensor.transpose(pc[:, 0:4],
                                     s_wir[0:4, dp * 128:(dp + 1) * 128],
                                     s_ident[0:4, 0:4])
                    gwt.inc(tensor.transpose(
                        pc[:, 4:8],
                        s_wfr[0:4, dp * 128:(dp + 1) * 128],
                        s_ident[0:4, 0:4]), 1)

        if L0DEBUG:
            wgd = P.take(["wgd_d"])[0]
            with (
                nc.sbuf_tensor("dwgf", [128, DP * 8], F32) as s_dwgf,
                nc.Block() as blk,
            ):
                @blk.vector
                def _(vector):
                    wgd.inc(vector.tensor_copy(s_dwgf[:, :], s_wg[:, :]), 1)

                @blk.sync
                def _(sync):
                    sync.wait_ge(wgd.h, 1)
                    wgd.inc(sync.dma_start(out=dwg[:, :], in_=s_dwgf[:, :]),
                            16)
                    sync.wait_ge(wgd.h, wgd.n)

        es_x = contextlib.ExitStack()
        s_xln = es_x.enter_context(
            nc.sbuf_tensor("xln", [128, DP * S], BF16, side="right"))
        emit_load_cm(nc, P, "ld", xin, S // 128, s_xln, s_ident)
        emit_ln_cm(nc, P, "ln1", s_xln, S, eps5, s_ones_c, s_ones_r,
                   dst=s_xln)

        gs = P.take(["gt_m", "gt_e", "gt_a", "gt_s", "gt_t", "gt_c"])
        gm, ge, ga, gsc, gt_, gc = gs
        with (
            nc.sbuf_tensor("rB", [4, S], F32) as s_rB,
            nc.sbuf_tensor("rC", [4, S], F32) as s_rC,
            nc.psum_tensor("gt_pi", [4, 512], F32) as pgi,
            nc.psum_tensor("gt_pf", [4, 512], F32) as pgf,
            nc.psum_tensor("gt_pt", [128, 4], F32) as pgt,
            nc.Block() as blk,
        ):
            gdmp = P.take(["gdmp_d"])[0]
            if L0DEBUG:
                @blk.sync
                def _(sync):
                    sync.wait_ge(ge.h, 16)
                    gdmp.inc(sync.dma_start(out=drows2[0:4, :],
                                            in_=s_rA[0:4, :]), 16)
                    gdmp.inc(sync.dma_start(out=drows2[4:8, :],
                                            in_=s_rB[0:4, :]), 16)

            @blk.tensor
            def _(tensor):
                for tc in range(S // 512):
                    if tc >= 1:
                        tensor.wait_ge(ge.h, 2 * tc)
                    for dpi in range(DP):
                        ins = tensor.matmul(
                            pgi[:, :], s_wg[:, dpi * 8:dpi * 8 + 4],
                            s_xln[:, dpi * S + tc * 512:
                                  dpi * S + tc * 512 + 512],
                            start=(dpi == 0), stop=(dpi == DP - 1))
                    gm.inc(ins, 1)
                    for dpi in range(DP):
                        ins = tensor.matmul(
                            pgf[:, :], s_wg[:, dpi * 8 + 4:dpi * 8 + 8],
                            s_xln[:, dpi * S + tc * 512:
                                  dpi * S + tc * 512 + 512],
                            start=(dpi == 0), stop=(dpi == DP - 1))
                    gm.inc(ins, 1)
                for jt in range(32):
                    tensor.wait_ge(gsc.h, 3)
                    if jt >= 1:
                        tensor.wait_ge(gc.h, jt)
                    gt_.inc(tensor.transpose(pgt[:, :],
                                             s_rE[0:4, jt * 128:(jt + 1) * 128],
                                             s_ident[0:4, 0:4]), 1)

            @blk.vector
            def _(vector):
                for tc in range(S // 512):
                    vector.wait_ge(gm.h, 2 * tc + 1)
                    ge.inc(vector.tensor_scalar(
                        s_rA[0:4, tc * 512:tc * 512 + 512], pgi[:, :],
                        s_bi4[:, :], None, op0=ALU.add), 1)
                    vector.wait_ge(gm.h, 2 * tc + 2)
                    ge.inc(vector.tensor_scalar(
                        s_rB[0:4, tc * 512:tc * 512 + 512], pgf[:, :],
                        s_bf4[:, :], None, op0=ALU.add), 1)
                if L0DEBUG:
                    vector.wait_ge(gdmp.h, 32)
                A, Bv, C, Dv, Ev = (s_rA[0:4, :], s_rB[0:4, :], s_rC[0:4, :],
                                    s_rD[0:4, :], s_rE[0:4, :])
                vector.tensor_scalar(C, Bv, -1.0, None, op0=ALU.mult)
                gsc.inc(vector.tensor_tensor(Dv, Bv, C, ALU.min), 1)
                vector.wait_ge(ga.h, 2)
                vector.tensor_scalar(C, Bv, 0.0, None, op0=ALU.min)
                vector.tensor_tensor(Bv, C, Dv, ALU.subtract)
                vector.tensor_tensor_scan(Dv, Bv, C, 0.0,
                                          op0=ALU.add, op1=ALU.bypass)
                gsc.inc(vector.tensor_tensor(Ev, A, Dv, ALU.subtract), 1)
                vector.tensor_tensor_scan(C, Ev, A, -3.0e38,
                                          op0=ALU.max, op1=ALU.bypass)
                vector.tensor_tensor(Bv, Dv, C, ALU.add)
                gsc.inc(vector.tensor_scalar(A, C, -1.0, None,
                                             op0=ALU.mult), 1)
                for jt in range(32):
                    vector.wait_ge(gt_.h, jt + 1)
                    gc.inc(vector.tensor_copy(
                        s_gcols[:, jt * 4:(jt + 1) * 4], pgt[:, :]), 1)
                vector.wait_ge(ga.h, 3)

            @blk.scalar
            def _(scalar):
                scalar.wait_ge(gsc.h, 1)
                ga.inc(scalar.activation(s_rC[0:4, :], s_rD[0:4, :],
                                         AF.Exp), 1)
                scalar.wait_ge(ga.h, 1)
                ga.inc(scalar.activation(s_rD[0:4, :], s_rC[0:4, :],
                                         AF.Ln, bias=one_c4), 1)
                scalar.wait_ge(gsc.h, 3)
                ga.inc(scalar.activation(s_rD[0:4, :], s_rB[0:4, :],
                                         AF.Exp, scale=-1.0), 1)

        hs = P.take(["hd_pj", "hd_pje", "hd_aqk", "hd_adt", "hd_act",
                     "hd_acv", "hd_est", "hd_est2", "hd_sde", "hd_sdf"])
        pj, pje, aqk, adt, act_, acv, est, est2, sde, sdf = hs
        with (
            nc.sbuf_tensor("qh", [128, 2 * S], BF16) as s_qh,
            nc.sbuf_tensor("kh", [128, 2 * S], BF16) as s_kh,
            nc.sbuf_tensor("vh", [128, 32 * 257], BF16) as s_vh,
            nc.sbuf_tensor("sbi", [128, 512], F32) as s_bi,
            nc.sbuf_tensor("sdt", [128, 2 * 512], F32) as s_dt,
            nc.sbuf_tensor("sct", [128, 2 * 512], BF16) as s_ct,
            nc.sbuf_tensor("sht", [128, 1024], F32) as s_ht,
            nc.sbuf_tensor("ssqe", [128, 1024], BF16) as s_sqe,
            nc.sbuf_tensor("snr", [4, 512], F32) as s_nr,
            nc.sbuf_tensor("srec", [4, 1024], F32) as s_rec,
            nc.sbuf_tensor("srbf", [128, 512], F32) as s_rbf,
            nc.sbuf_tensor("erow", [1, 1024], F32) as s_erow,
            nc.sbuf_tensor("sso", [128, 2 * 1024], BF16) as s_so2,
        ):
            with nc.Block() as blk:
                @blk.gpsimd
                def _(gp):
                    vones = s_vh.ap().rearrange("p (t c) -> p t c", c=257)
                    gp.memset(vones[:, :, 256:257], 1.0)

            for hh in range(NH0):
                _emit_l0attn_head(nc, es, hh, dict(
                    xln=s_xln, wqp=s_wqp, wkp=s_wkp, wvp=s_wvp,
                    qh=s_qh, kh=s_kh, vh=s_vh, bi=s_bi, dt=s_dt, ct=s_ct,
                    ht=s_ht, sqe=s_sqe, nr=s_nr, rec=s_rec, rbf=s_rbf,
                    erow=s_erow, so2=s_so2, tril=s_tril, sel4=s_sel4,
                    oc4=s_oc4, gcols=s_gcols, nrm4=s_rA, em4=s_rD, gam=s_gam,
                    ones_c=s_ones_c, ones_cf=s_ones_cf, ones_r=s_ones_r,
                    cpool=s_cpool, sout=sout,
                ), hs)

        es_x.close()

        if L0DEBUG:
            dbs = P.take(["db_d"])[0]
            with nc.Block() as blk:
                @blk.sync
                def _(sync):
                    dbs.inc(sync.dma_start(out=drows[0:4, :],
                                           in_=s_rA[0:4, :]), 16)
                    dbs.inc(sync.dma_start(out=drows[4:8, :],
                                           in_=s_rD[0:4, :]), 16)
                    dbs.inc(sync.dma_start(out=drows[8:12, :],
                                           in_=s_rE[0:4, :]), 16)
                    sync.wait_ge(dbs.h, dbs.n)

                @blk.gpsimd
                def _(gp):
                    gp.dma_start(out=dsout[:, :], in_=sout[:, :]).then_inc(
                        dbs.h, 16)
                    gp.wait_ge(dbs.h, dbs.n + 16)
            dbs.n += 16
        l2s = P.take(["l2_l"])[0]
        with nc.sbuf_tensor("so", [128, DP * S], BF16) as s_so:
            with nc.Block() as blk:
                @blk.sync
                def _(sync):
                    for t in range(DP):
                        l2s.inc(sync.dma_start(
                            out=s_so[:, t * S:(t + 1) * S],
                            in_=sout[t * 128:(t + 1) * 128, :]), 16)
                    sync.wait_ge(l2s.h, l2s.n)
            emit_ln_cm(nc, P, "ln2", s_so, S, eps5, s_ones_c, s_ones_r,
                       dst=s_so)
            emit_store_stream(nc, P, "st", s_so, xin, xout, S // 128,
                              s_ident)
    return nc


def _pack_blockdiag(w):
    out = np.zeros((8, 128, 128), np.float32)
    for t in range(8):
        for a in range(32):
            gp = 32 * t + a
            out[t, 4 * a:4 * a + 4, 4 * a:4 * a + 4] = w[gp].T
    return np.ascontiguousarray(out.reshape(1024, 128))


_TRIL = np.triu(np.ones((128, 128), np.float32))
_SEL4 = np.zeros((4, 512), np.float32)
for _h in range(4):
    _SEL4[_h, _h * 128:(_h + 1) * 128] = 1.0
_OC4 = np.zeros((128, 16), np.float32)
for _h in range(4):
    _OC4[:, _h * 4 + _h] = 1.0


def l0attn_inmaps(Xstate, wq, wk, wv, wi, bi, wf, bf, ln_w):
    wqp = _pack_blockdiag(wq)
    wkp = _pack_blockdiag(wk)
    wvp = _pack_blockdiag(wv)
    maps = []
    for c in range(NCORE):
        b = c // 2
        maps.append({
            "xin": np.ascontiguousarray(Xstate[b]),
            "wqp": wqp, "wkp": wkp, "wvp": wvp,
            "wi_d": wi.astype(np.float32),
            "wf_d": wf.astype(np.float32),
            "bi_d": bi.reshape(4, 1).astype(np.float32),
            "bf_d": bf.reshape(4, 1).astype(np.float32),
            "cvecs": ln_w.reshape(1, D).astype(np.float32),
            "tril": _TRIL, "sel4": _SEL4, "oc4": _OC4, "ident": _IDENT,
        })
    return maps


def run_l0attn(Xstate, *args):
    nc = build_l0attn()
    res = run_bass_kernel_spmd(nc, l0attn_inmaps(Xstate, *args),
                               core_ids=list(range(NCORE)))
    out = np.empty_like(Xstate)
    for c in range(NCORE):
        b, h = c // 2, c % 2
        out[b, h * ROWS:(h + 1) * ROWS] = \
            res.results[c]["xout"][h * ROWS:(h + 1) * ROWS]
    return out


XT = 2048 + 128
FUSED_UPTO = 99
FUSED_EXTRA = True


def _dstripe(xres, t0, n, dp):
    return xres.ap()[t0:t0 + n, dp].rearrange("n p c -> p n c")


def _sb3(sb_slice):
    return sb_slice.rearrange("p (n c) -> p n c", c=128)


def emit_stripe_update(nc, P, tag, s_so, xres_in, xres_out, nt,
                       so_off=0, blend_flag=None, startA=0, startB=0,
                       out_start=0):
    ldA, ldB, ev, st = P.take(["su_a", "su_b", "su_e", "su_s"])
    a0, b0, e0, s0 = ldA.n, ldB.n, ev.n, st.n
    so_ntok = s_so.shape[1] // DP
    W = nt * 128
    with (
        nc.sbuf_tensor(f"{tag}_ia", [128, 2 * W], F32) as s_ia,
        nc.sbuf_tensor(f"{tag}_ib", [128, 2 * W], F32) as s_ib,
        nc.sbuf_tensor(f"{tag}_o", [128, 2 * W], F32) as s_o,
        nc.Block() as blk,
    ):
        @blk.sync
        def _(sync):
            for dp in range(DP):
                sl = (dp % 2) * W
                if dp >= 2:
                    sync.wait_ge(ev.h, e0 + (dp - 1))
                ldA.inc(sync.dma_start(
                    out=_sb3(s_ia.ap()[:, sl:sl + W]),
                    in_=_dstripe(xres_in, startA, nt, dp)), 16)
                if blend_flag is not None:
                    ldB.inc(sync.dma_start(
                        out=_sb3(s_ib.ap()[:, sl:sl + W]),
                        in_=_dstripe(xres_in, startB, nt, dp)), 16)

        @blk.vector
        def _(vector):
            for dp in range(DP):
                sl = (dp % 2) * W
                vector.wait_ge(ldA.h, a0 + 16 * (dp + 1))
                if blend_flag is not None:
                    vector.wait_ge(ldB.h, b0 + 16 * (dp + 1))
                    f_col, omf_col = blend_flag
                    for tc in range((W + 511) // 512):
                        c = sl + tc * 512
                        w = min(512, W - tc * 512)
                        vector.tensor_scalar(s_ib[:, c:c + w],
                                             s_ib[:, c:c + w], f_col, None,
                                             op0=ALU.mult)
                        vector.scalar_tensor_tensor(
                            s_ia[:, c:c + w], s_ia[:, c:c + w], omf_col,
                            s_ib[:, c:c + w], op0=ALU.mult, op1=ALU.add)
                if dp >= 2:
                    vector.wait_ge(st.h, s0 + 16 * (dp - 1))
                last = None
                for tc in range((W + 511) // 512):
                    c = sl + tc * 512
                    w = min(512, W - tc * 512)
                    last = vector.tensor_tensor(
                        s_o[:, c:c + w], s_ia[:, c:c + w],
                        s_so[:, dp * so_ntok + so_off + tc * 512:
                             dp * so_ntok + so_off + tc * 512 + w], ALU.add)
                ev.inc(last, 1)

        @blk.gpsimd
        def _(gp):
            for dp in range(DP):
                sl = (dp % 2) * W
                gp.wait_ge(ev.h, e0 + dp + 1)
                st.inc(gp.dma_start(
                    out=_dstripe(xres_out, out_start, nt, dp),
                    in_=_sb3(s_o.ap()[:, sl:sl + W])), 16)
            gp.wait_ge(st.h, st.n)

def emit_blend_load(nc, P, tag, xres_in, s_dst, nt, f_col, omf_col,
                    startA=0, dst_off=0):
    ldA, ldB, ev = P.take(["bl_a", "bl_b", "bl_e"])
    a0, b0, e0 = ldA.n, ldB.n, ev.n
    ntin = xres_in.shape[0]
    W = nt * 128
    dst_ntok = s_dst.shape[1] // DP
    with (
        nc.sbuf_tensor(f"{tag}_ba", [128, 2 * W], BF16) as s_a,
        nc.sbuf_tensor(f"{tag}_bb", [128, 2 * W], BF16) as s_b,
        nc.Block() as blk,
    ):
        idxA = [(startA + i) % ntin for i in range(nt)]
        idxB = [(startA + i + ntin // 2) % ntin for i in range(nt)]

        def runs(idx):
            out = []
            s = 0
            while s < len(idx):
                e = s
                while (e + 1 < len(idx) and idx[e + 1] == idx[e] + 1
                       and e - s + 1 < 16):
                    e += 1
                out.append((s, idx[s], e - s + 1))
                s = e + 1
            return out

        @blk.gpsimd
        def _(gp):
            for dp in range(DP):
                sl = (dp % 2) * W
                if dp >= 2:
                    gp.wait_ge(ev.h, e0 + dp - 1)
                for (pos, t0, n) in runs(idxA):
                    ldA.inc(gp.dma_start(
                        out=_sb3(s_a.ap()[:, sl + pos * 128:
                                          sl + (pos + n) * 128]),
                        in_=_dstripe(xres_in, t0, n, dp)), 16)
                for (pos, t0, n) in runs(idxB):
                    ldB.inc(gp.dma_start(
                        out=_sb3(s_b.ap()[:, sl + pos * 128:
                                          sl + (pos + n) * 128]),
                        in_=_dstripe(xres_in, t0, n, dp)), 16)

        nA = len(runs(idxA))
        nB = len(runs(idxB))

        @blk.vector
        def _(vector):
            for dp in range(DP):
                sl = (dp % 2) * W
                vector.wait_ge(ldA.h, a0 + 16 * nA * (dp + 1))
                vector.wait_ge(ldB.h, b0 + 16 * nB * (dp + 1))
                last = None
                for tc in range((W + 511) // 512):
                    c = sl + tc * 512
                    w = min(512, W - tc * 512)
                    vector.tensor_scalar(s_b[:, c:c + w], s_b[:, c:c + w],
                                         f_col, None, op0=ALU.mult)
                    last = vector.scalar_tensor_tensor(
                        s_dst[:, dp * dst_ntok + dst_off + tc * 512:
                              dp * dst_ntok + dst_off + tc * 512 + w],
                        s_a[:, c:c + w], omf_col, s_b[:, c:c + w],
                        op0=ALU.mult, op1=ALU.add)
                ev.inc(last, 1)

def build_fused():
    nc = bass.Bass("TRN2", target_bir_lowering=False,
                   detect_race_conditions=False)
    xin = nc.dram_tensor("xin", [S, D], F32, kind="ExternalInput")
    wqp = nc.dram_tensor("wqp", [D, 128], F32, kind="ExternalInput")
    wkp = nc.dram_tensor("wkp", [D, 128], F32, kind="ExternalInput")
    wvp = nc.dram_tensor("wvp", [D, 128], F32, kind="ExternalInput")
    wi_d = nc.dram_tensor("wi_d", [4, HID], F32, kind="ExternalInput")
    wf_d = nc.dram_tensor("wf_d", [4, HID], F32, kind="ExternalInput")
    bi_d = nc.dram_tensor("bi_d", [4, 1], F32, kind="ExternalInput")
    bf_d = nc.dram_tensor("bf_d", [4, 1], F32, kind="ExternalInput")
    tril = nc.dram_tensor("tril", [128, 128], F32, kind="ExternalInput")
    sel4 = nc.dram_tensor("sel4", [4, 512], F32, kind="ExternalInput")
    oc4 = nc.dram_tensor("oc4", [128, 16], F32, kind="ExternalInput")
    wgT = nc.dram_tensor("wgT", [D, HID], I32, kind="ExternalInput")
    wuT = nc.dram_tensor("wuT", [D, HID], I32, kind="ExternalInput")
    wdT = nc.dram_tensor("wdT", [HID, D], I32, kind="ExternalInput")
    wq2 = nc.dram_tensor("wq2", [D, D], F32, kind="ExternalInput")
    wk2 = nc.dram_tensor("wk2", [D, D], F32, kind="ExternalInput")
    wv2 = nc.dram_tensor("wv2", [D, D], F32, kind="ExternalInput")
    wo2 = nc.dram_tensor("wo2", [D, D], F32, kind="ExternalInput")
    bvrow = nc.dram_tensor("bvrow", [1, 1024], F32, kind="ExternalInput")
    wrT = nc.dram_tensor("wrT", [D, D], F32, kind="ExternalInput")
    wkT5 = nc.dram_tensor("wkT5", [D, D], F32, kind="ExternalInput")
    wvT5 = nc.dram_tensor("wvT5", [D, D], F32, kind="ExternalInput")
    cvecs = nc.dram_tensor("cvecs", [12, 1024], F32, kind="ExternalInput")
    hflag = nc.dram_tensor("hflag", [128, 2], F32, kind="ExternalInput")
    ident = nc.dram_tensor("ident", [128, 128], F32, kind="ExternalInput")
    xout = nc.dram_tensor("xout", [ROWS, D], F32, kind="ExternalOutput")
    NT32 = S // 128
    xres0 = nc.dram_tensor("xres0_scr", [NT32, DP, 128, 128], F32)
    xres1 = nc.dram_tensor("xres1_scr", [NT32, DP, 128, 128], F32)
    xres2 = nc.dram_tensor("xres2_scr", [NT32, DP, 128, 128], F32)
    xres3 = nc.dram_tensor("xres3_scr", [17, DP, 128, 128], F32)
    sout = nc.dram_tensor("sout_scr", [D, S], BF16)

    with contextlib.ExitStack() as es:
        E = es.enter_context
        P = SemPool(nc, es)
        s_ident = E(nc.sbuf_tensor("s_ident", [128, 128], F32))
        s_ones_c = E(nc.sbuf_tensor("ones_c", [128, 1], BF16))
        s_ones_cf = E(nc.sbuf_tensor("ones_cf", [128, 1], F32))
        s_ones_r = E(nc.sbuf_tensor("ones_r", [1, 128], F32))
        s_ones65 = E(nc.sbuf_tensor("ones65", [65, 128], F32))
        s_cpool = E(nc.sbuf_tensor("cpool", [128, 4], F32))
        s_cvc = E(nc.sbuf_tensor("cvc", [128, DP * 12], F32))
        s_gam = E(nc.sbuf_tensor("gam", [128, DP], F32))
        s_hf = E(nc.sbuf_tensor("hf", [128, 2], F32))

        emit_consts(nc, P, s_ident, ident, s_ones_c, s_ones_r,
                    s_cpool, [1e-5, 1e-3, 1.0, 1e-8])
        eps5 = s_cpool[0:1, 0:1]
        eps3 = s_cpool[0:1, 1:2]
        one_c4 = s_cpool[0:4, 2:3]
        emit_cvecs(nc, P, "cv", cvecs, s_cvc, 12, 12, s_ident)

        def cvcol(dp, j):
            return s_cvc[:, dp * 12 + j:dp * 12 + j + 1]

        f_col = s_hf[:, 0:1]
        omf_col = s_hf[:, 1:2]

        def _scope(name, _st=[None]):
            if _st[0] is not None:
                nc.leave_named_scope(_st[0][0], _st[0][1], False)
                _st[0] = None
            if name is not None:
                _st[0] = (name, nc.enter_named_scope(name, False)[0])

        _scope("P1_load")
        aux = P("fx_aux")
        with nc.Block() as blk:
            @blk.sync
            def _(sync):
                aux.inc(sync.dma_start(out=s_hf[:, :], in_=hflag[:, :]), 16)
                sync.wait_ge(aux.h, aux.n)

            @blk.gpsimd
            def _(gp):
                gp.memset(s_ones_cf[:, :], 1.0)
                gp.memset(s_ones65[:, :], 1.0)

            @blk.vector
            def _(vector):
                for dp in range(DP):
                    vector.tensor_scalar(s_gam[:, dp:dp + 1],
                                         s_cvc[:, dp * 12:dp * 12 + 1],
                                         1.0, None, op0=ALU.add)

        if FUSED_UPTO < 20:
            return nc
        _scope("P2_attn")
        with (
            nc.sbuf_tensor("p2_tril", [128, 128], F32) as s_tril,
            nc.sbuf_tensor("p2_sel4", [4, 512], F32) as s_sel4,
            nc.sbuf_tensor("p2_oc4", [128, 16], BF16) as s_oc4,
            nc.sbuf_tensor("p2_wqp", [128, 8 * 128], BF16) as s_wqp,
            nc.sbuf_tensor("p2_wkp", [128, 8 * 128], BF16) as s_wkp,
            nc.sbuf_tensor("p2_wvp", [128, 8 * 128], BF16) as s_wvp,
            nc.sbuf_tensor("p2_wg", [128, DP * 8], BF16) as s_wg,
            nc.sbuf_tensor("p2_bi4", [4, 1], F32) as s_bi4,
            nc.sbuf_tensor("p2_bf4", [4, 1], F32) as s_bf4,
            nc.sbuf_tensor("p2_rA", [4, S], F32) as s_rA,
            nc.sbuf_tensor("p2_rD", [4, S], F32) as s_rD,
            nc.sbuf_tensor("p2_rE", [4, S], F32) as s_rE,
            nc.sbuf_tensor("p2_gcols", [128, 32 * 4], F32) as s_gcols,
        ):
            auxs, auxg = P.take(["aux_a", "aux_b"])
            with nc.Block() as blk:
                a0, g0 = auxs.n, auxg.n

                @blk.sync
                def _(sync):
                    auxs.inc(sync.dma_start(out=s_tril[:, :],
                                            in_=tril[:, :]), 16)
                    auxs.inc(sync.dma_start(out=s_sel4[:, :],
                                            in_=sel4[:, :]), 16)
                    auxs.inc(sync.dma_start(out=s_bi4[:, :],
                                            in_=bi_d[:, :]), 16)
                    auxs.inc(sync.dma_start(out=s_bf4[:, :],
                                            in_=bf_d[:, :]), 16)
                    sync.wait_ge(auxs.h, auxs.n)

                @blk.gpsimd
                def _(gp):
                    auxg.inc(gp.dma_start(out=s_oc4[:, :], in_=oc4[:, :]), 16)
                    auxg.inc(gp.dma_start(
                        out=s_wqp.ap().rearrange("p (a c) -> p a c", c=128),
                        in_=wqp.ap().rearrange("(a p) c -> p a c", p=128)), 16)
                    auxg.inc(gp.dma_start(
                        out=s_wkp.ap().rearrange("p (a c) -> p a c", c=128),
                        in_=wkp.ap().rearrange("(a p) c -> p a c", p=128)), 16)
                    auxg.inc(gp.dma_start(
                        out=s_wvp.ap().rearrange("p (a c) -> p a c", c=128),
                        in_=wvp.ap().rearrange("(a p) c -> p a c", p=128)), 16)
                    gp.wait_ge(auxg.h, auxg.n)

            gws = P.take(["gw_l", "gw_t", "gw_e"])
            gwl, gwt, gwe = gws
            gwl0, gwt0, gwe0 = gwl.n, gwt.n, gwe.n
            with (
                nc.sbuf_tensor("p2_wirow", [4, HID], F32) as s_wir,
                nc.sbuf_tensor("p2_wfrow", [4, HID], F32) as s_wfr,
                nc.psum_tensor("p2_gw_pc", [128, 8], F32) as pc,
                nc.Block() as blk,
            ):
                @blk.sync
                def _(sync):
                    gwl.inc(sync.dma_start(out=s_wir[:, :], in_=wi_d[:, :]),
                            16)
                    gwl.inc(sync.dma_start(out=s_wfr[:, :], in_=wf_d[:, :]),
                            16)

                @blk.vector
                def _(vector):
                    vector.wait_ge(gwl.h, gwl0 + 32)
                    for rows in (s_wir, s_wfr):
                        vector.tensor_tensor(rows[:, 0:1024], rows[:, 0:1024],
                                             rows[:, 1024:2048], ALU.add)
                        gwe.inc(vector.tensor_tensor(
                            rows[:, 0:1024], rows[:, 0:1024],
                            rows[:, 2048:3072], ALU.add), 1)
                    for dp in range(DP):
                        vector.wait_ge(gwt.h, gwt0 + dp + 1)
                        gwe.inc(vector.tensor_copy(
                            s_wg[:, dp * 8:(dp + 1) * 8], pc[:, 0:8]), 1)

                @blk.tensor
                def _(tensor):
                    tensor.wait_ge(gwe.h, gwe0 + 2)
                    for dp in range(DP):
                        if dp >= 1:
                            tensor.wait_ge(gwe.h, gwe0 + 2 + dp)
                        tensor.transpose(pc[:, 0:4],
                                         s_wir[0:4, dp * 128:(dp + 1) * 128],
                                         s_ident[0:4, 0:4])
                        gwt.inc(tensor.transpose(
                            pc[:, 4:8],
                            s_wfr[0:4, dp * 128:(dp + 1) * 128],
                            s_ident[0:4, 0:4]), 1)

            es_x = contextlib.ExitStack()
            s_xln = es_x.enter_context(
                nc.sbuf_tensor("p2_xln", [128, DP * S], BF16, side="right"))
            emit_load_cm(nc, P, "p2ld", xin, S // 128, s_xln, s_ident,
                         xres_dram=xres0)
            emit_ln_cm(nc, P, "p2ln1", s_xln, S, eps5, s_ones_c, s_ones_r,
                       dst=s_xln)

            gs = P.take(["gt_m", "gt_e", "gt_a", "gt_s", "gt_t", "gt_c"])
            gm, ge, ga, gsc, gt_, gc = gs
            gm0, ge0, ga0, gsc0, gt0, gc0 = (gm.n, ge.n, ga.n, gsc.n, gt_.n,
                                             gc.n)
            with (
                nc.sbuf_tensor("p2_rB", [4, S], F32) as s_rB,
                nc.sbuf_tensor("p2_rC", [4, S], F32) as s_rC,
                nc.psum_tensor("p2_gt_pi", [4, 512], F32) as pgi,
                nc.psum_tensor("p2_gt_pf", [4, 512], F32) as pgf,
                nc.psum_tensor("p2_gt_pt", [128, 4], F32) as pgt,
                nc.Block() as blk,
            ):
                @blk.tensor
                def _(tensor):
                    for tc in range(S // 512):
                        if tc >= 1:
                            tensor.wait_ge(ge.h, ge0 + 2 * tc)
                        for dpi in range(DP):
                            ins = tensor.matmul(
                                pgi[:, :], s_wg[:, dpi * 8:dpi * 8 + 4],
                                s_xln[:, dpi * S + tc * 512:
                                      dpi * S + tc * 512 + 512],
                                start=(dpi == 0), stop=(dpi == DP - 1))
                        gm.inc(ins, 1)
                        for dpi in range(DP):
                            ins = tensor.matmul(
                                pgf[:, :], s_wg[:, dpi * 8 + 4:dpi * 8 + 8],
                                s_xln[:, dpi * S + tc * 512:
                                      dpi * S + tc * 512 + 512],
                                start=(dpi == 0), stop=(dpi == DP - 1))
                        gm.inc(ins, 1)
                    for jt in range(32):
                        tensor.wait_ge(gsc.h, gsc0 + 3)
                        if jt >= 1:
                            tensor.wait_ge(gc.h, gc0 + jt)
                        gt_.inc(tensor.transpose(
                            pgt[:, :], s_rE[0:4, jt * 128:(jt + 1) * 128],
                            s_ident[0:4, 0:4]), 1)

                @blk.vector
                def _(vector):
                    for tc in range(S // 512):
                        vector.wait_ge(gm.h, gm0 + 2 * tc + 1)
                        ge.inc(vector.tensor_scalar(
                            s_rA[0:4, tc * 512:tc * 512 + 512], pgi[:, :],
                            s_bi4[:, :], None, op0=ALU.add), 1)
                        vector.wait_ge(gm.h, gm0 + 2 * tc + 2)
                        ge.inc(vector.tensor_scalar(
                            s_rB[0:4, tc * 512:tc * 512 + 512], pgf[:, :],
                            s_bf4[:, :], None, op0=ALU.add), 1)
                    A, Bv, C, Dv, Ev = (s_rA[0:4, :], s_rB[0:4, :],
                                        s_rC[0:4, :], s_rD[0:4, :],
                                        s_rE[0:4, :])
                    vector.tensor_scalar(C, Bv, -1.0, None, op0=ALU.mult)
                    gsc.inc(vector.tensor_tensor(Dv, Bv, C, ALU.min), 1)
                    vector.wait_ge(ga.h, ga0 + 2)
                    vector.tensor_scalar(C, Bv, 0.0, None, op0=ALU.min)
                    vector.tensor_tensor(Bv, C, Dv, ALU.subtract)
                    vector.tensor_tensor_scan(Dv, Bv, C, 0.0,
                                              op0=ALU.add, op1=ALU.bypass)
                    gsc.inc(vector.tensor_tensor(Ev, A, Dv, ALU.subtract), 1)
                    vector.tensor_tensor_scan(C, Ev, A, -3.0e38,
                                              op0=ALU.max, op1=ALU.bypass)
                    vector.tensor_tensor(Bv, Dv, C, ALU.add)
                    gsc.inc(vector.tensor_scalar(A, C, -1.0, None,
                                                 op0=ALU.mult), 1)
                    for jt in range(32):
                        vector.wait_ge(gt_.h, gt0 + jt + 1)
                        gc.inc(vector.tensor_copy(
                            s_gcols[:, jt * 4:(jt + 1) * 4], pgt[:, :]), 1)
                    vector.wait_ge(ga.h, ga0 + 3)

                @blk.scalar
                def _(scalar):
                    scalar.wait_ge(gsc.h, gsc0 + 1)
                    ga.inc(scalar.activation(s_rC[0:4, :], s_rD[0:4, :],
                                             AF.Exp), 1)
                    scalar.wait_ge(ga.h, ga0 + 1)
                    ga.inc(scalar.activation(s_rD[0:4, :], s_rC[0:4, :],
                                             AF.Ln, bias=one_c4), 1)
                    scalar.wait_ge(gsc.h, gsc0 + 3)
                    ga.inc(scalar.activation(s_rD[0:4, :], s_rB[0:4, :],
                                             AF.Exp, scale=-1.0), 1)

            hs = P.take(["hd_pj", "hd_pje", "hd_aqk", "hd_adt", "hd_act",
                         "hd_acv", "hd_est", "hd_est2", "hd_sde", "hd_sdf"])
            es_h = contextlib.ExitStack()
            Eh = es_h.enter_context
            s_qh = Eh(nc.sbuf_tensor("p2_qh", [128, 2 * S], BF16))
            s_kh = Eh(nc.sbuf_tensor("p2_kh", [128, 2 * S], BF16))
            s_vh = Eh(nc.sbuf_tensor("p2_vh", [128, 32 * 257], BF16))
            s_bi = Eh(nc.sbuf_tensor("p2_sbi", [128, 512], F32))
            s_dt = Eh(nc.sbuf_tensor("p2_sdt", [128, 2 * 512], F32))
            s_ct = Eh(nc.sbuf_tensor("p2_sct", [128, 2 * 512], BF16))
            s_ht = Eh(nc.sbuf_tensor("p2_sht", [128, 1024], F32))
            s_sqe = Eh(nc.sbuf_tensor("p2_ssqe", [128, 1024], BF16))
            s_nr = Eh(nc.sbuf_tensor("p2_snr", [4, 512], F32))
            s_rec = Eh(nc.sbuf_tensor("p2_srec", [4, 1024], F32))
            s_rbf = Eh(nc.sbuf_tensor("p2_srbf", [128, 512], F32))
            s_erow = Eh(nc.sbuf_tensor("p2_erow", [1, 1024], F32))
            s_so2 = Eh(nc.sbuf_tensor("p2_sso", [128, 2 * 1024], BF16))
            with nc.Block() as blk:
                @blk.gpsimd
                def _(gp):
                    vones = s_vh.ap().rearrange("p (t c) -> p t c", c=257)
                    gp.memset(vones[:, :, 256:257], 1.0)

            for hh in range(NH0):
                _emit_l0attn_head(nc, es, hh, dict(
                    xln=s_xln, wqp=s_wqp, wkp=s_wkp, wvp=s_wvp,
                    qh=s_qh, kh=s_kh, vh=s_vh, bi=s_bi, dt=s_dt, ct=s_ct,
                    ht=s_ht, sqe=s_sqe, nr=s_nr, rec=s_rec, rbf=s_rbf,
                    erow=s_erow, so2=s_so2, tril=s_tril, sel4=s_sel4,
                    oc4=s_oc4, gcols=s_gcols, nrm4=s_rA, em4=s_rD,
                    gam=s_gam, ones_c=s_ones_c, ones_cf=s_ones_cf,
                    ones_r=s_ones_r, cpool=s_cpool, sout=sout,
                ), hs)
            es_h.close()
            es_x.close()

        _scope("P2_sw")
        l2s = P("p2_l2l")
        with nc.sbuf_tensor("p2_so", [128, DP * S], BF16) as s_so:
            l20 = l2s.n
            with nc.Block() as blk:
                @blk.sync
                def _(sync):
                    for t in range(DP):
                        l2s.inc(sync.dma_start(
                            out=s_so[:, t * S:(t + 1) * S],
                            in_=sout[t * 128:(t + 1) * 128, :]), 16)
                    sync.wait_ge(l2s.h, l2s.n)
            emit_ln_cm(nc, P, "p2ln2", s_so, S, eps5, s_ones_c, s_ones_r,
                       dst=s_so)
            emit_stripe_update(nc, P, "p2su", s_so, xres0, xres1, NT32)

        if FUSED_UPTO < 30:
            return nc
        _scope("P3_ffn")
        for ph in range(2):
            with nc.sbuf_tensor(f"p3_x{ph}", [128, DP * ROWS], BF16,
                                side="right") as s_x3:
                ld3 = P("p3_ld")
                with nc.Block() as blk:
                    @blk.gpsimd
                    def _(gp):
                        for dp in range(DP):
                            ld3.inc(gp.dma_start(
                                out=_sb3(s_x3.ap()[:, dp * ROWS:
                                                   (dp + 1) * ROWS]),
                                in_=_dstripe(xres1, ph * 16, 16, dp)), 16)
                        gp.wait_ge(ld3.h, ld3.n)
                emit_ln_cm(nc, P, f"p3ln1{ph}", s_x3, ROWS, eps5, s_ones_c,
                           s_ones_r, dst=s_x3)
                with nc.sbuf_tensor(f"p3_ffn{ph}", [128, DP * ROWS],
                                    BF16) as s_f3:
                    emit_l0ffn_body(nc, P, s_x3, s_f3, wgT, wuT, wdT,
                                    ntok=ROWS, tag=f"f3{ph}")
                    emit_ln_cm(nc, P, f"p3ln2{ph}", s_f3, ROWS, eps5,
                               s_ones_c, s_ones_r, dst=s_f3)
                    emit_stripe_update(nc, P, f"p3su{ph}", s_f3, xres1,
                                       xres2, 16, startA=ph * 16,
                                       out_start=ph * 16)

        if FUSED_UPTO < 40:
            return nc
        _scope("P4_pre")
        with (
            nc.sbuf_tensor("p4_z", [128, DP * XT], BF16, side="right") as s_z,
        ):
            es_x4 = contextlib.ExitStack()
            s_xln4 = es_x4.enter_context(
                nc.sbuf_tensor("p4_xln", [128, DP * S], BF16, side="right"))
            with (
                nc.sbuf_tensor("p4_wq", [128, DP * 1024], BF16) as s_wq4,
                nc.sbuf_tensor("p4_wk", [128, DP * 1024], BF16) as s_wk4,
                nc.sbuf_tensor("p4_wv", [128, DP * 1024], BF16) as s_wv4,
            ):
                wl = P("p4_wl")
                with nc.Block() as blk:
                    @blk.gpsimd
                    def _(gp):
                        emit_wload(nc, gp, wq2, s_wq4, wl)
                        emit_wload(nc, gp, wk2, s_wk4, wl)
                        emit_wload(nc, gp, wv2, s_wv4, wl)
                        gp.wait_ge(wl.h, wl.n)
                if FUSED_UPTO < 41:
                    es_x4.close()
                    return nc
                emit_blend_load(nc, P, "p4bl", xres2, s_xln4, NT32,
                                f_col, omf_col)
                if FUSED_UPTO < 42:
                    es_x4.close()
                    return nc
                emit_ln_cm(nc, P, "p4ln1", s_xln4, S, eps5, s_ones_c,
                           s_ones_r, dst=s_xln4)
                if FUSED_UPTO < 43:
                    es_x4.close()
                    return nc

                es_g4 = contextlib.ExitStack()
                Eg = es_g4.enter_context
                s_bvr = Eg(nc.sbuf_tensor("p4_bvr", [1, 1024], F32))
                bvl = P("p4_bv")
                with nc.Block() as blk:
                    @blk.sync
                    def _(sync):
                        bvl.inc(sync.dma_start(out=s_bvr[:, :],
                                               in_=bvrow[:, :]), 16)
                        sync.wait_ge(bvl.h, bvl.n)
                s_kg = Eg(nc.sbuf_tensor("p4_kg", [128, 2 * S], BF16))
                s_qg = Eg(nc.sbuf_tensor("p4_qg", [128, 2 * XT], BF16))
                s_vg = Eg(nc.sbuf_tensor("p4_vg", [128, 32 * 260], BF16))
                s_pt = Eg(nc.sbuf_tensor("p4_pt", [128, 2 * 1024], BF16))
                s_sqs = Eg(nc.sbuf_tensor("p4_sqs", [128, 1024], BF16))
                s_mrow = Eg(nc.sbuf_tensor("p4_mrow", [1, 8], F32))
                s_negm = Eg(nc.sbuf_tensor("p4_negm", [128, 1], F32))
                s_bvb = Eg(nc.sbuf_tensor("p4_bvb", [128, 256], F32))
                s_lrec = Eg(nc.sbuf_tensor("p4_lrec", [65, 1024], F32))
                s_rb = Eg(nc.sbuf_tensor("p4_rb", [128, 1024], F32))
                with nc.Block() as blk:
                    @blk.gpsimd
                    def _(gp):
                        vones = s_vg.ap().rearrange(
                            "p (t h c) -> p t h c", h=GH, c=65)
                        gp.memset(vones[:, :, :, 64:65], 1.0)

                sems4 = P.take(["at_pj", "at_pje", "at_bve", "at_sqa",
                                "at_smm", "at_rmx", "at_nsel", "at_bsm",
                                "at_bex", "at_bpv", "at_lns", "at_bcm",
                                "at_rbe", "at_bze"])
                bufs4 = {"xln": s_xln4, "wq": s_wq4, "wk": s_wk4,
                         "wv": s_wv4, "kg": s_kg, "qg": s_qg,
                         "vg": s_vg, "pt": s_pt, "sqs": s_sqs,
                         "mrow": s_mrow, "negm": s_negm, "bvb": s_bvb,
                         "lrec": s_lrec, "rb": s_rb, "z": s_z,
                         "ones_c": s_ones_c, "ones_r": s_ones_r,
                         "ones65": s_ones65, "bvr": s_bvr,
                         "cvcol": lambda dp, j: cvcol(dp, j + 1)}
                _scope("P4_groups")
                for g in range(NG):
                    _emit_l1attn_group(nc, es, g, bufs4, sems4,
                                       qw=XT, zw=XT, extra=FUSED_EXTRA)
                es_g4.close()
                _scope("P4_wo")
            if FUSED_UPTO < 44:
                es_x4.close()
                return nc

            with nc.sbuf_tensor("p4_y", [128, DP * XT], BF16) as s_y:
                with nc.sbuf_tensor("p4_wo", [128, DP * 1024], BF16) as s_wo:
                    wl2 = P("p4_wl2")
                    with nc.Block() as blk:
                        @blk.gpsimd
                        def _(gp):
                            emit_wload(nc, gp, wo2, s_wo, wl2)
                            gp.wait_ge(wl2.h, wl2.n)

                    wsem, wsc, wev = P.take(["wo_m", "wo_s", "wo_e"])
                    wm0, ws0, we0 = wsem.n, wsc.n, wev.n
                    wchunks = [(0, 512), (512, 512), (1024, 512),
                               (1536, 512), (2048, 128)]
                    with (
                        nc.sbuf_tensor("p4_wot", [128, 2048], F32) as s_wot,
                        nc.psum_tensor("p4_wo_pd", [128, 2048], F32) as pd,
                        nc.Block() as blk,
                    ):
                        @blk.tensor
                        def _(tensor):
                            nj = 0
                            for dpo in range(DP):
                                for (o, w) in wchunks:
                                    par = (nj % 4) * 512
                                    if nj >= 4:
                                        tensor.wait_ge(wsc.h, ws0 + nj - 3)
                                    for hkp in range(DP):
                                        ins = tensor.matmul(
                                            pd[:, par:par + w],
                                            s_wo[:, hkp * 1024 + dpo * 128:
                                                 hkp * 1024 + (dpo + 1) * 128],
                                            s_z[:, hkp * XT + o:
                                                hkp * XT + o + w],
                                            start=(hkp == 0),
                                            stop=(hkp == DP - 1))
                                    wsem.inc(ins, 1)
                                    nj += 1

                        @blk.scalar
                        def _(scalar):
                            nj = 0
                            for dpo in range(DP):
                                for (o, w) in wchunks:
                                    par = (nj % 4) * 512
                                    scalar.wait_ge(wsem.h, wm0 + nj + 1)
                                    if nj >= 4:
                                        scalar.wait_ge(wev.h, we0 + nj - 3)
                                    wsc.inc(scalar.activation(
                                        s_wot[:, par:par + w],
                                        pd[:, par:par + w],
                                        AF.Identity, bias=cvcol(dpo, 3)), 1)
                                    nj += 1

                        @blk.vector
                        def _(vector):
                            nj = 0
                            for dpo in range(DP):
                                for (o, w) in wchunks:
                                    par = (nj % 4) * 512
                                    vector.wait_ge(wsc.h, ws0 + nj + 1)
                                    wev.inc(vector.tensor_tensor(
                                        s_y[:, dpo * XT + o:dpo * XT + o + w],
                                        s_wot[:, par:par + w],
                                        s_xln4[:, dpo * S + o:dpo * S + o + w]
                                        if o < 2048 else
                                        s_xln4[:, dpo * S + S - 128:
                                               dpo * S + S],
                                        ALU.add), 1)
                                    nj += 1

                es_x4.close()
                if FUSED_UPTO < 45:
                    return nc
                gcols4 = [cvcol(dp, 4) for dp in range(DP)]
                bcols4 = [cvcol(dp, 5) for dp in range(DP)]
                emit_ln_cm(nc, P, "p4lni", s_y, XT, eps3, s_ones_c, s_ones_r,
                           dst=s_y, gamma_cols=gcols4, beta_cols=bcols4)
                emit_ln_cm(nc, P, "p4ln2", s_y, XT, eps5, s_ones_c, s_ones_r,
                           dst=s_y)
                emit_stripe_update(nc, P, "p4su", s_y, xres2, xres3, 16,
                                   so_off=0, blend_flag=(f_col, omf_col),
                                   startA=0, startB=16, out_start=0)
                emit_stripe_update(nc, P, "p4sx", s_y, xres2, xres3, 1,
                                   so_off=2048, blend_flag=(f_col, omf_col),
                                   startA=31, startB=15, out_start=16)

        if FUSED_UPTO < 50:
            return nc
        _scope("P5_ffn")
        XT5 = XT
        es_xr5 = contextlib.ExitStack()
        s_xr = es_xr5.enter_context(
            nc.sbuf_tensor("p5_xr", [128, DP * ROWS], BF16, side="right"))
        es_x5 = contextlib.ExitStack()
        s_x5 = es_x5.enter_context(
            nc.sbuf_tensor("p5_x", [128, DP * XT5], BF16, side="right"))
        if True:
            ld5 = P("p5_ld")
            with nc.Block() as blk:
                @blk.gpsimd
                def _(gp):
                    for dp in range(DP):
                        ld5.inc(gp.dma_start(
                            out=_sb3(s_x5.ap()[:, dp * XT5:(dp + 1) * XT5]),
                            in_=_dstripe(xres3, 0, 17, dp)), 16)
                    gp.wait_ge(ld5.h, ld5.n)
            emit_ln_cm(nc, P, "p5ln1", s_x5, XT5, eps5, s_ones_c, s_ones_r,
                       dst=s_x5)

            def mu_r(dp):
                return cvcol(dp, 6)

            def mu_k(dp):
                return cvcol(dp, 7)

            def om_r(dp):
                return s_cvc[:, dp * 12 + 8:dp * 12 + 9]

            def om_k(dp):
                return s_cvc[:, dp * 12 + 9:dp * 12 + 10]

            with nc.sbuf_tensor("p5_ffn", [128, DP * ROWS], BF16) as s_f5:
                with nc.sbuf_tensor("p5_xk", [128, DP * ROWS], BF16) as s_xk:
                    with (
                        nc.sbuf_tensor("p5_mt", [128, 512], F32) as s_mt,
                        nc.Block() as blk,
                    ):
                        @blk.vector
                        def _(vector):
                            for dp in range(DP):
                                vector.tensor_scalar(
                                    s_cvc[:, dp * 12 + 8:dp * 12 + 10],
                                    s_cvc[:, dp * 12 + 6:dp * 12 + 8],
                                    -1.0, 1.0, op0=ALU.mult, op1=ALU.add)
                            for dp in range(DP):
                                vector.tensor_scalar(
                                    s_mt[:, 0:1],
                                    s_x5[:, dp * XT5 + 2175:dp * XT5 + 2176],
                                    f_col, None, op0=ALU.mult)
                                for (mu, om, dstb) in ((mu_r, om_r, s_xr),
                                                       (mu_k, om_k, s_xk)):
                                    vector.tensor_scalar(
                                        s_mt[:, 1:2], s_mt[:, 0:1], om(dp),
                                        None, op0=ALU.mult)
                                    vector.scalar_tensor_tensor(
                                        dstb[:, dp * ROWS:dp * ROWS + 1],
                                        s_x5[:, dp * XT5:dp * XT5 + 1],
                                        mu(dp), s_mt[:, 1:2],
                                        op0=ALU.mult, op1=ALU.add)
                                for tc in range(ROWS // 512):
                                    lo = 1 if tc == 0 else 0
                                    cur = s_x5[:, dp * XT5 + tc * 512 + lo:
                                               dp * XT5 + tc * 512 + 512]
                                    prv = s_x5[:, dp * XT5 + tc * 512 + lo - 1:
                                               dp * XT5 + tc * 512 + 511]
                                    for (mu, om, dstb) in ((mu_r, om_r, s_xr),
                                                           (mu_k, om_k, s_xk)):
                                        vector.tensor_scalar(
                                            s_mt[:, lo:512], prv, om(dp),
                                            None, op0=ALU.mult)
                                        vector.scalar_tensor_tensor(
                                            dstb[:, dp * ROWS + tc * 512 + lo:
                                                 dp * ROWS + tc * 512 + 512],
                                            cur, mu(dp), s_mt[:, lo:512],
                                            op0=ALU.mult, op1=ALU.add)

                    es_x5.close()
                    with (
                        nc.sbuf_tensor("p5_wr", [128, DP * 1024], BF16) as s_wr,
                        nc.sbuf_tensor("p5_wk", [128, DP * 1024], BF16) as s_wk5,
                        nc.sbuf_tensor("p5_wv", [128, DP * 1024], BF16) as s_wv5,
                    ):
                        wl5 = P("p5_wl")
                        with nc.Block() as blk:
                            @blk.gpsimd
                            def _(gp):
                                emit_wload(nc, gp, wrT, s_wr, wl5)
                                emit_wload(nc, gp, wkT5, s_wk5, wl5)
                                emit_wload(nc, gp, wvT5, s_wv5, wl5)
                                gp.wait_ge(wl5.h, wl5.n)

                        with nc.sbuf_tensor("p5_r", [128, DP * ROWS],
                                            BF16) as s_r:
                            emit_proj(nc, P, "p5pr", s_wr, s_xr, s_r, ROWS,
                                      "sigmoid")
                            es_xr5.close()
                            with nc.sbuf_tensor("p5_ka", [128, DP * ROWS],
                                                BF16) as s_ka:
                                emit_proj(nc, P, "p5pk", s_wk5, s_xk, s_ka,
                                          ROWS, "relusq")
                                emit_proj(nc, P, "p5pv", s_wv5, s_ka, s_f5,
                                          ROWS, "rmult", rbuf=s_r)

                with nc.sbuf_tensor("p5_fin", [128, DP * ROWS], F32) as s_fin:
                    ld6 = P("p5_ld6")
                    with nc.Block() as blk:
                        @blk.gpsimd
                        def _(gp):
                            for dp in range(DP):
                                ld6.inc(gp.dma_start(
                                    out=_sb3(s_fin.ap()[:, dp * ROWS:
                                                        (dp + 1) * ROWS]),
                                    in_=_dstripe(xres3, 0, 16, dp)), 16)
                            gp.wait_ge(ld6.h, ld6.n)
                    emit_ln_cm(nc, P, "p5ln2", s_f5, ROWS, eps5, s_ones_c,
                               s_ones_r, resid=s_fin)
                    emit_store_tm(nc, P, "p5st", s_fin, xout, ROWS // 128,
                                  s_ident)
        _scope(None)
    return nc


def fused_inmaps(inputs):
    wqp = _pack_blockdiag(inputs["L0_attn_wq"])
    wkp = _pack_blockdiag(inputs["L0_attn_wk"])
    wvp = _pack_blockdiag(inputs["L0_attn_wv"])
    wgT = np.ascontiguousarray(inputs["L0_ffn_w_gate"].T)
    wuT = np.ascontiguousarray(inputs["L0_ffn_w_up"].T)
    wdT = np.ascontiguousarray(inputs["L0_ffn_w_down"].T)
    wq2 = np.ascontiguousarray(inputs["L1_attn_Wq"].reshape(D, D))
    wk2 = np.ascontiguousarray(inputs["L1_attn_Wk"].reshape(D, D))
    wv2 = np.ascontiguousarray(inputs["L1_attn_Wv"].reshape(D, D))
    wo2 = np.ascontiguousarray(inputs["L1_attn_Wo"].reshape(D, D))
    wrT = np.ascontiguousarray(inputs["L1_ffn_W_r"].T)
    wkT5 = np.ascontiguousarray(inputs["L1_ffn_W_k"].T)
    wvT5 = np.ascontiguousarray(inputs["L1_ffn_W_v"].T)
    cv = np.zeros((12, 1024), np.float32)
    cv[0] = inputs["L0_attn_ln_w"]
    cv[1] = inputs["L1_attn_bq"].reshape(-1)
    cv[2] = inputs["L1_attn_bk"].reshape(-1)
    cv[3] = inputs["L1_attn_bo"]
    cv[4] = inputs["L1_attn_gamma"]
    cv[5] = inputs["L1_attn_beta"]
    cv[6] = inputs["L1_ffn_mu_r"]
    cv[7] = inputs["L1_ffn_mu_k"]
    bvr = inputs["L1_attn_bv"].reshape(1, D).astype(np.float32)
    X = np.asarray(inputs["L0_attn_x"], np.float32)
    maps = []
    for c in range(NCORE):
        b, h = c // 2, c % 2
        hf = np.zeros((128, 2), np.float32)
        hf[:, 0] = float(h)
        hf[:, 1] = 1.0 - float(h)
        maps.append({
            "xin": np.ascontiguousarray(X[b]),
            "wqp": wqp, "wkp": wkp, "wvp": wvp,
            "wi_d": inputs["L0_attn_wi"].astype(np.float32),
            "wf_d": inputs["L0_attn_wf"].astype(np.float32),
            "bi_d": inputs["L0_attn_bi"].reshape(4, 1).astype(np.float32),
            "bf_d": inputs["L0_attn_bf"].reshape(4, 1).astype(np.float32),
            "tril": _TRIL, "sel4": _SEL4, "oc4": _OC4, "ident": _IDENT,
            "wgT": wgT, "wuT": wuT, "wdT": wdT,
            "wq2": wq2, "wk2": wk2, "wv2": wv2, "wo2": wo2, "bvrow": bvr,
            "wrT": wrT, "wkT5": wkT5, "wvT5": wvT5,
            "cvecs": cv, "hflag": hf,
        })
    return maps


def kernel_fused(**inputs):
    nc = build_fused()
    res = run_bass_kernel_spmd(nc, fused_inmaps(inputs),
                               core_ids=list(range(NCORE)))
    X = np.asarray(inputs["L0_attn_x"], np.float32)
    out = np.empty_like(X)
    for c in range(NCORE):
        b, h = c // 2, c % 2
        out[b, h * ROWS:(h + 1) * ROWS] = res.results[c]["xout"]
    return out


def kernel(**inputs):
    return kernel_fused(**inputs).astype(np.float32)


if __name__ == "__main__":
    pass
